# version 16
# baseline (speedup 1.0000x reference)
import numpy as np
import concourse.bass as bass
import concourse.mybir as mybir
from concourse.bass_utils import run_bass_kernel_spmd
from contextlib import ExitStack

F32 = mybir.dt.float32
BF16 = mybir.dt.bfloat16
AF = mybir.ActivationFunctionType
OP = mybir.AluOpType

NCORES = 8
B, S, D = 2, 4096, 2048
TPC = 1024
HALF = 512
KD = D // 128
INTER = 5632
KI = INTER // 128
H = 16
CH = 64
NCH = TPC // CH
NCHH = HALF // CH
NCORR = 4
LNEPS = 1e-5
RMSEPS = 1.1920928955078125e-07
CLAMP = 1e-6

STAGE = 3


class Sem:
    def __init__(self, h):
        self.h = h
        self.n = 0

    def inc(self, instr, k=1):
        instr.then_inc(self.h, k)
        self.n += k
        return self.n


class DSlot:
    def __init__(self, sem):
        self.s = sem

    def dma(self, eng, **kw):
        return self.s.inc(eng.dma_start(**kw), 16)

    def wait(self, eng, tok):
        eng.wait_ge(self.s.h, tok)


def build_program():
    nc = bass.Bass("TRN2", target_bir_lowering=False, debug=False)
    x_in = nc.dram_tensor("x", [D, TPC], F32, kind="ExternalInput").ap()
    wgu = nc.dram_tensor("wgu", [KI, 128, 2 * D], F32, kind="ExternalInput").ap()
    wd = nc.dram_tensor("wd", [KD, 128, INTER], F32, kind="ExternalInput").ap()
    wkva = nc.dram_tensor("wkva", [H, 128, 3 * D], F32, kind="ExternalInput").ap()
    wq_d = nc.dram_tensor("wq", [H, 128, D], F32, kind="ExternalInput").ap()
    wg_d = nc.dram_tensor("wgp", [H, 128, D], F32, kind="ExternalInput").ap()
    wo_d = nc.dram_tensor("wo", [KD, 128, D], F32, kind="ExternalInput").ap()
    gs_d = nc.dram_tensor("gs", [128, KI], F32, kind="ExternalInput").ap()
    us_d = nc.dram_tensor("us", [128, KI], F32, kind="ExternalInput").ap()
    ds_d = nc.dram_tensor("ds", [128, KD], F32, kind="ExternalInput").ap()
    msk_d = nc.dram_tensor("msk", [128, 16], F32, kind="ExternalInput").ap()
    ind16_d = nc.dram_tensor("ind16", [128, 16 * H], F32, kind="ExternalInput").ap()
    indr_d = nc.dram_tensor("indr", [16, 128 * H], F32, kind="ExternalInput").ap()
    out_d = nc.dram_tensor("out", [D, TPC], F32, kind="ExternalOutput").ap()
    x1sp = nc.dram_tensor("x1sp", [D, TPC], F32).ap()
    x2sp = nc.dram_tensor("x2sp", [D, TPC], F32).ap()
    wkva_b = nc.dram_tensor("wkva_b", [H, 128, 3 * D], BF16).ap()
    wq_b = nc.dram_tensor("wq_b", [H, 128, D], BF16).ap()
    wg_b = nc.dram_tensor("wg_b", [H, 128, D], BF16).ap()
    wo_b = nc.dram_tensor("wo_b", [KD, 128, D], BF16).ap()
    cc_in = nc.dram_tensor("cc_in", [2, D], F32)
    cc_out = nc.dram_tensor("cc_out", [2 * NCORES, D], F32)

    es = ExitStack()
    sbt = lambda name, shape, dt: es.enter_context(nc.sbuf_tensor(name, shape, dt))
    pst = lambda name, shape: es.enter_context(nc.psum_tensor(name, shape, F32))
    mksem = lambda name: Sem(es.enter_context(nc.semaphore(name)))
    mkslot = lambda name: DSlot(mksem(name))

    xn = sbt("xn", [128, 16384], BF16)
    AR = sbt("AR", [128, 53248], BF16)
    S1 = sbt("S1", [128, 1024], F32)
    S2 = sbt("S2", [128, 1024], F32)
    S3 = sbt("S3", [128, 1024], F32)
    S4 = sbt("S4", [128, 1024], F32)
    S5 = sbt("S5", [128, 1024], F32)
    E1 = sbt("E1", [128, 512], F32)
    E2 = sbt("E2", [128, 512], F32)
    E3 = sbt("E3", [128, 512], F32)
    E4 = sbt("E4", [128, 512], F32)
    E5 = sbt("E5", [128, 512], BF16)
    ST1 = sbt("ST1", [128, 1024], F32)
    ST2 = sbt("ST2", [128, 1024], F32)
    WB = sbt("WB", [128, 8192], BF16)
    r_mean = sbt("r_mean", [1, 1024], F32)
    r_rstd = sbt("r_rstd", [1, 1024], F32)
    RQ16 = sbt("RQ16", [16, 512], F32)
    RQ16b = sbt("RQ16b", [16, 512], BF16)
    ones_col = sbt("ones_col", [128, 1], BF16)
    ones_row = sbt("ones_row", [1, 128], F32)
    b2col = sbt("b2col", [128, 1], F32)
    epscol = sbt("epscol", [128, 1], F32)
    gsT = sbt("gsT", [128, KI], F32)
    usT = sbt("usT", [128, KI], F32)
    dsT = sbt("dsT", [128, KD], F32)
    mskT = sbt("mskT", [128, 16], F32)
    ind16 = sbt("ind16s", [128, 16 * H], BF16)
    indr = sbt("indrs", [16, 128 * H], BF16)
    AcA = sbt("AcA", [128, 16], F32)
    AcB2 = sbt("AcB2", [128, 16], F32)
    scol = sbt("scol", [128, 16], F32)
    sin_t = sbt("sin", [128, 16], F32)
    gab = sbt("gab", [128, 256], F32)
    T1 = sbt("T1", [128, 16], F32)
    T2 = sbt("T2", [128, 16], F32)

    PA = pst("PA", [128, 1024])
    PB = pst("PB", [128, 1024])
    PC = pst("PC", [128, 1024])
    PD = pst("PD", [128, 1024])

    sPE = mksem("sPE")
    sDV = mksem("sDV")
    sSE = mksem("sSE")
    sPO = mksem("sPO")
    sCC = mksem("sCC")
    STS = [mkslot("sST0"), mkslot("sST1")]
    WSL = [mkslot("sWA"), mkslot("sWB")]
    DSL = [mkslot("sDA"), mkslot("sDB")]
    KSL = [mkslot("sKA"), mkslot("sKB")]
    QSL = mkslot("sQ")
    GSL = mkslot("sG")
    OSL = [mkslot("sOA"), mkslot("sOB")]
    CW = mkslot("sCW")
    CP = mkslot("sCP")
    CAT = mkslot("sCT")
    CAT2 = mkslot("sCT2")
    WOS = [mkslot("sWO0"), mkslot("sWO1")]

    TE, PO, SY = nc.tensor, nc.gpsimd, nc.sync

    class OEng:
        def __init__(self, eng, sem):
            self._e = eng
            self._s = sem

        def wait_ge(self, h, v):
            self._e.wait_ge(h, v)

        def dma_start(self, **kw):
            return self._e.dma_start(**kw)

        def __getattr__(self, name):
            f = getattr(self._e, name)

            def wrapped(*a, **k):
                self._e.wait_ge(self._s.h, self._s.n)
                return self._s.inc(f(*a, **k))
            return wrapped

    VE = OEng(nc.vector, sDV)
    SEn = OEng(nc.scalar, sSE)
    PL = OEng(nc.gpsimd, sPO)

    a_f32 = AR[:, 0:16384].bitcast(F32)
    u_bf = AR[:, 16384:24576]
    w1_bf = AR[:, 24576:32768]
    w2_bf = AR[:, 32768:40960]
    y_bf = AR[:, 40960:49152]
    R_bf = AR[:, 49152:53248]
    hdn = AR[:, 0:KI * 1024]
    corr_bf = u_bf[:, 0:4096]
    l1v = AR[:, 45056:46080]
    l2v = AR[:, 46080:47104]
    kvaA = WB[:, 0:6144]
    kvaB = xn[:, 8192:14336]
    qbuf = WB[:, 6144:8192]
    gbuf = xn[:, 14336:16384]
    dwb = [xn[:, 0:5632], xn[:, 5632:11264]]
    wob = [qbuf, gbuf]
    wosl = WOS
    st_bufs = [ST1, ST2]

    with nc.Block() as blk:
        VE.memset(ones_col[:, :], 1.0)
        VE.memset(ones_row[:, :], 1.0)
        VE.memset(b2col[:, :], 2.0)
        VE.memset(epscol[:, :], LNEPS)
        VE.memset(scol[:, :], 0.0)
        VE.memset(AcA[:, :], 1.0)
        VE.memset(AcB2[:, :], 1.0)
        (VE.memset(sin_t[:, :], 0.0))
        CW.dma(SY, out=gsT[:, :], in_=gs_d)
        CW.dma(SY, out=usT[:, :], in_=us_d)
        CW.dma(SY, out=dsT[:, :], in_=ds_d)
        consts_w = CW.dma(SY, out=mskT[:, :], in_=msk_d)
        CP.dma(PO, out=ind16[:, :], in_=ind16_d)
        consts_p = CP.dma(PO, out=indr[:, :], in_=indr_d)

        def ln_stats_full(src_dram):
            SY.wait_ge(sDV.h, sDV.n)
            VE.wait_ge(sPE.h, sPE.n)
            toks = [STS[0].dma(SY, out=st_bufs[0][:, :], in_=src_dram[0:128, :]),
                    STS[1].dma(SY, out=st_bufs[1][:, :], in_=src_dram[128:256, :])]
            casts = []
            smms = []
            for g in range(KD):
                STS[g % 2].wait(VE, toks[g])
                if g > 0:
                    VE.wait_ge(sPE.h, smms[g - 1])
                cst = VE.tensor_copy(xn[:, g * 1024:(g + 1) * 1024], st_bufs[g % 2][:, :])
                if g > 0:
                    SEn.wait_ge(sPE.h, smms[g - 1])
                STS[g % 2].wait(SEn, toks[g])
                casts.append(SEn.activation(l2v[:, :], st_bufs[g % 2][:, :], AF.Square))
                TE.wait_ge(sDV.h, cst)
                TE.wait_ge(sSE.h, casts[g])
                mm = None
                for c0 in (0, 512):
                    TE.matmul(PC[0:1, c0:c0 + 512], ones_col[:, :],
                              xn[:, g * 1024 + c0:g * 1024 + c0 + 512],
                              start=(g == 0), stop=(g == KD - 1))
                    mm = TE.matmul(PD[0:1, c0:c0 + 512], ones_col[:, :], l2v[:, c0:c0 + 512],
                                   start=(g == 0), stop=(g == KD - 1))
                smms.append(sPE.inc(mm))
                if g + 2 < KD:
                    SY.wait_ge(sDV.h, cst)
                    SY.wait_ge(sSE.h, casts[g])
                    toks.append(STS[g % 2].dma(
                        SY, out=st_bufs[g % 2][:, :],
                        in_=src_dram[(g + 2) * 128:(g + 3) * 128, :]))
            return smms[-1]

        def ln_rows(pe_done):
            VE.wait_ge(sPE.h, pe_done)
            VE.wait_ge(OSL[0].s.h, OSL[0].s.n)
            VE.wait_ge(OSL[1].s.h, OSL[1].s.n)
            VE.tensor_scalar(r_mean[:, :], PC[0:1, :], 1.0 / D, 0.0, OP.mult, OP.add)
            VE.tensor_scalar(S3[0:1, :], PD[0:1, :], 1.0 / D, 0.0, OP.mult, OP.add)
            VE.tensor_tensor(S4[0:1, :], r_mean[:, :], r_mean[:, :], OP.mult)
            (VE.tensor_tensor(S3[0:1, :], S3[0:1, :], S4[0:1, :], OP.subtract))
            SEn.wait_ge(sDV.h, sDV.n)
            SEn.activation(S4[0:1, :], S3[0:1, :], AF.Ln, bias=epscol[0:1, 0:1])
            return (SEn.activation(r_rstd[:, :], S4[0:1, :], AF.Exp, scale=-0.5))

        def ln_bcast(se_rows, c0=0, n=1024, guard_dv=None):
            TE.wait_ge(sSE.h, se_rows)
            if guard_dv is not None:
                TE.wait_ge(sDV.h, guard_dv)
            mm = None
            for cc in range(0, n, 512):
                TE.matmul(PC[:, cc:cc + 512], ones_row[:, :], r_mean[0:1, c0 + cc:c0 + cc + 512],
                          start=True, stop=True)
                mm = TE.matmul(PD[:, cc:cc + 512], ones_row[:, :], r_rstd[0:1, c0 + cc:c0 + cc + 512],
                               start=True, stop=True)
            pe = sPE.inc(mm)
            VE.wait_ge(sPE.h, pe)
            VE.tensor_copy(S1[:, 0:n], PC[:, 0:n])
            return (VE.tensor_copy(S2[:, 0:n], PD[:, 0:n]))

        def ln_apply_full(src_dram, dv_bc):
            dv_tok = po_tok = None
            for g in range(KD):
                if g == 0:
                    VE.wait_ge(sDV.h, dv_bc)
                    PL.wait_ge(sDV.h, dv_bc)
                    PL.wait_ge(sPE.h, sPE.n)
                if g % 2 == 0:
                    VE.tensor_tensor(S3[:, :], xn[:, g * 1024:(g + 1) * 1024], S1[:, :], OP.subtract)
                    dv_tok = VE.tensor_tensor(
                        xn[:, g * 1024:(g + 1) * 1024], S3[:, :], S2[:, :], OP.mult)
                else:
                    PL.tensor_tensor(S4[:, :], xn[:, g * 1024:(g + 1) * 1024], S1[:, :], OP.subtract)
                    po_tok = PL.tensor_tensor(
                        xn[:, g * 1024:(g + 1) * 1024], S4[:, :], S2[:, :], OP.mult)
            return (dv_tok, po_tok)

        def ffn_phase(xsrc_dram, xdst_dram, dv_xn, fuse_ln, wtok=None, casts=None):
            casts = list(casts or [])
            SEn.wait_ge(CW.s.h, consts_w)
            VE.wait_ge(CW.s.h, consts_w)
            if wtok is None:
                PO.wait_ge(sPE.h, sPE.n)
                wtok = [WSL[0].dma(PO, out=WB[:, 0:4096], in_=wgu[0, :, :]),
                        WSL[1].dma(PO, out=WB[:, 4096:8192], in_=wgu[1, :, :])]
            gu_mm, gu_evac = [], []
            for m in range(KI):
                pg = PA if (m % 2 == 0) else PC
                pu = PB if (m % 2 == 0) else PD
                wbuf = WB[:, (m % 2) * 4096:(m % 2) * 4096 + 4096]
                WSL[m % 2].wait(TE, wtok[m])
                if m == 0:
                    TE.wait_ge(sDV.h, dv_xn[0])
                    TE.wait_ge(sPO.h, dv_xn[1])
                if m >= 2:
                    TE.wait_ge(sDV.h, gu_evac[m - 2])
                mm = None
                for k in range(KD):
                    for cc in (0, 512):
                        TE.matmul(pg[:, cc:cc + 512], wbuf[:, k * 128:k * 128 + 128],
                                  xn[:, k * 1024 + cc:k * 1024 + cc + 512],
                                  start=(k == 0), stop=(k == KD - 1))
                for k in range(KD):
                    for cc in (0, 512):
                        mm = TE.matmul(pu[:, cc:cc + 512], wbuf[:, 2048 + k * 128:2048 + k * 128 + 128],
                                       xn[:, k * 1024 + cc:k * 1024 + cc + 512],
                                       start=(k == 0), stop=(k == KD - 1))
                gu_mm.append(sPE.inc(mm))
                if m + 2 < KI:
                    PO.wait_ge(sPE.h, gu_mm[m])
                    wtok.append(WSL[m % 2].dma(
                        PO, out=WB[:, (m % 2) * 4096:(m % 2) * 4096 + 4096],
                        in_=wgu[m + 2, :, :]))
                for _ in range(2):
                    if casts:
                        dst, src = casts.pop(0)
                        CAT.dma(PO, out=dst, in_=src)
                sg = [S4, S5][m % 2]
                SEn.wait_ge(sPE.h, gu_mm[m])
                if m >= 2:
                    SEn.wait_ge(sDV.h, gu_evac[m - 2])
                sse_m = (SEn.activation(sg[:, :], pg[:, :], AF.Silu, scale=gsT[:, m:m + 1]))
                VE.wait_ge(sPE.h, gu_mm[m])
                VE.tensor_scalar(S3[:, :], pu[:, :], usT[:, m:m + 1], 0.0, OP.mult, OP.add)
                VE.wait_ge(sSE.h, sse_m)
                gu_evac.append((VE.tensor_tensor(
                    hdn[:, m * 1024:(m + 1) * 1024], sg[:, :], S3[:, :], OP.mult)))
            hdn_ready = gu_evac[-1]
            PO.wait_ge(sPE.h, gu_mm[KI - 1])
            dtok = [DSL[0].dma(PO, out=dwb[0][:, :], in_=wd[0, :, :]),
                    DSL[1].dma(PO, out=dwb[1][:, :], in_=wd[1, :, :])]
            SY.wait_ge(sDV.h, sDV.n)
            xtok = [STS[0].dma(SY, out=st_bufs[0][:, :], in_=xsrc_dram[0:128, :]),
                    STS[1].dma(SY, out=st_bufs[1][:, :], in_=xsrc_dram[128:256, :])]
            dn_mm, dn_evac, dn_st, fmms = [], [], [], []
            for m2 in range(KD):
                pd_ = PA if (m2 % 2 == 0) else PB
                DSL[m2 % 2].wait(TE, dtok[m2])
                if m2 == 0:
                    TE.wait_ge(sDV.h, hdn_ready)
                if m2 >= 2:
                    TE.wait_ge(sDV.h, dn_evac[m2 - 2])
                mm = None
                for k in range(KI):
                    for cc in (0, 512):
                        mm = TE.matmul(pd_[:, cc:cc + 512], dwb[m2 % 2][:, k * 128:k * 128 + 128],
                                       hdn[:, k * 1024 + cc:k * 1024 + cc + 512],
                                       start=(k == 0), stop=(k == KI - 1))
                dn_mm.append(sPE.inc(mm))
                if m2 + 2 < KD:
                    PO.wait_ge(sPE.h, dn_mm[m2])
                    dtok.append(DSL[m2 % 2].dma(PO, out=dwb[m2 % 2][:, :], in_=wd[m2 + 2, :, :]))
                ob = [S4, S5][m2 % 2]
                VE.wait_ge(sPE.h, dn_mm[m2])
                STS[m2 % 2].wait(VE, xtok[m2])
                OSL[m2 % 2].wait(VE, OSL[m2 % 2].s.n)
                VE.tensor_scalar(S3[:, :], pd_[:, :], dsT[:, m2:m2 + 1], 0.5, OP.mult, OP.mult)
                dn_evac.append((VE.tensor_tensor(
                    ob[:, :], S3[:, :], st_bufs[m2 % 2][:, :], OP.add)))
                if m2 + 2 < KD:
                    SY.wait_ge(sDV.h, dn_evac[m2])
                    xtok.append(STS[m2 % 2].dma(
                        SY, out=st_bufs[m2 % 2][:, :],
                        in_=xsrc_dram[(m2 + 2) * 128:(m2 + 3) * 128, :]))
                SEn.wait_ge(sDV.h, dn_evac[m2])
                dn_st.append(OSL[m2 % 2].dma(
                    SEn, out=xdst_dram[m2 * 128:(m2 + 1) * 128, :], in_=ob[:, :]))
                if fuse_ln:
                    if m2 > 0:
                        VE.wait_ge(sPE.h, fmms[m2 - 1])
                    VE.tensor_copy(l1v[:, :], ob[:, :])
                    cst = (VE.tensor_tensor(l2v[:, :], ob[:, :], ob[:, :], OP.mult))
                    TE.wait_ge(sDV.h, cst)
                    mm = None
                    for c0 in (0, 512):
                        TE.matmul(PC[0:1, c0:c0 + 512], ones_col[:, :], l1v[:, c0:c0 + 512],
                                  start=(m2 == 0), stop=(m2 == KD - 1))
                        mm = TE.matmul(PD[0:1, c0:c0 + 512], ones_col[:, :], l2v[:, c0:c0 + 512],
                                       start=(m2 == 0), stop=(m2 == KD - 1))
                    fmms.append(sPE.inc(mm))
            while casts:
                dst, src = casts.pop(0)
                CAT.dma(PO, out=dst, in_=src)
            return fmms[-1] if fuse_ln else None

        def attn_apply(half, se_rows_ln2):
            cofs = half * 512
            TE.wait_ge(sSE.h, se_rows_ln2)
            TE.wait_ge(sDV.h, sDV.n)
            TE.matmul(PD[:, 0:512], ones_row[:, :], r_mean[0:1, cofs:cofs + 512],
                      start=True, stop=True)
            mm = TE.matmul(PD[:, 512:1024], ones_row[:, :], r_rstd[0:1, cofs:cofs + 512],
                           start=True, stop=True)
            pe = sPE.inc(mm)
            SEn.wait_ge(sPE.h, pe)
            SEn.wait_ge(sDV.h, sDV.n)
            SEn.copy(E3[:, :], PD[:, 0:512])
            rbtok = SEn.copy(E4[:, :], PD[:, 512:1024])
            SY.wait_ge(sDV.h, sDV.n)
            SY.wait_ge(sPO.h, sPO.n)
            toks = [STS[0].dma(SY, out=st_bufs[0][:, 0:512], in_=x1sp[0:128, cofs:cofs + 512]),
                    STS[1].dma(SY, out=st_bufs[1][:, 0:512], in_=x1sp[128:256, cofs:cofs + 512])]
            dv_tok = po_tok = None
            ptmp = E2 if half == 0 else S1
            for g in range(KD):
                if g == 0:
                    VE.wait_ge(sSE.h, rbtok)
                    PL.wait_ge(sSE.h, rbtok)
                    if half == 0:
                        OSL[0].wait(PL, OSL[0].s.n)
                        OSL[1].wait(PL, OSL[1].s.n)
                if g % 2 == 0:
                    STS[0].wait(VE, toks[g])
                    VE.tensor_tensor(E1[:, :], st_bufs[0][:, 0:512], E3[:, :], OP.subtract)
                    dv_tok = VE.tensor_tensor(
                        xn[:, g * 512:g * 512 + 512], E1[:, :], E4[:, :], OP.mult)
                else:
                    STS[1].wait(PL, toks[g])
                    PL.tensor_tensor(ptmp[:, 0:512], st_bufs[1][:, 0:512], E3[:, :], OP.subtract)
                    po_tok = PL.tensor_tensor(
                        xn[:, g * 512:g * 512 + 512], ptmp[:, 0:512], E4[:, :], OP.mult)
                if g + 2 < KD:
                    if g % 2 == 0:
                        SY.wait_ge(sDV.h, dv_tok)
                    else:
                        SY.wait_ge(sPO.h, po_tok)
                    toks.append(STS[g % 2].dma(
                        SY, out=st_bufs[g % 2][:, 0:512],
                        in_=x1sp[(g + 2) * 128:(g + 3) * 128, cofs:cofs + 512]))
            return (dv_tok, po_tok)

        def outproj_group(half, grp, y_token, first_wtk, W=128):
            cofs2 = half * 512 + grp * W
            wtk = first_wtk
            SY.wait_ge(sDV.h, sDV.n)
            xtk = [STS[0].dma(SY, out=st_bufs[0][:, 0:W], in_=x1sp[0:128, cofs2:cofs2 + W]),
                   STS[1].dma(SY, out=st_bufs[1][:, 0:W], in_=x1sp[128:256, cofs2:cofs2 + W])]
            op_mm, op_cp, op_evac, op_st = [], [], [], []
            for m2 in range(KD):
                pd_ = PB if (m2 % 2 == 0) else PC
                wosl[m2 % 2].wait(TE, wtk[m2])
                if m2 == 0:
                    TE.wait_ge(sPO.h, y_token)
                if m2 >= 2:
                    TE.wait_ge(sSE.h, op_cp[m2 - 2])
                mm = None
                for k in range(KD):
                    mm = TE.matmul(pd_[:, 0:W], wob[m2 % 2][:, k * 128:k * 128 + 128],
                                   y_bf[:, k * 512 + grp * W:k * 512 + grp * W + W],
                                   start=(k == 0), stop=(k == KD - 1))
                op_mm.append(sPE.inc(mm))
                if m2 + 2 < KD:
                    SEn.wait_ge(sPE.h, op_mm[m2])
                    wtk.append(wosl[m2 % 2].dma(SEn, out=wob[m2 % 2][:, :],
                                                in_=wo_b[m2 + 2, :, :]))
                ob = E2[:, (m2 % 2) * 256:(m2 % 2) * 256 + W]
                SEn.wait_ge(sPE.h, op_mm[m2])
                OSL[m2 % 2].wait(SEn, OSL[m2 % 2].s.n)
                op_cp.append(SEn.copy(ob, pd_[:, 0:W]))
                PL.wait_ge(sSE.h, op_cp[m2])
                STS[m2 % 2].wait(PL, xtk[m2])
                OSL[m2 % 2].wait(PL, OSL[m2 % 2].s.n)
                op_evac.append(PL.tensor_tensor(ob, ob, st_bufs[m2 % 2][:, 0:W], OP.add))
                if m2 + 2 < KD:
                    SY.wait_ge(sPO.h, op_evac[m2])
                    xtk.append(STS[m2 % 2].dma(
                        SY, out=st_bufs[m2 % 2][:, 0:W],
                        in_=x1sp[(m2 + 2) * 128:(m2 + 3) * 128, cofs2:cofs2 + W]))
                SEn.wait_ge(sPO.h, op_evac[m2])
                op_st.append(OSL[m2 % 2].dma(
                    SEn, out=x2sp[m2 * 128:(m2 + 1) * 128, cofs2:cofs2 + W], in_=ob))
            return op_evac[-1]

        def attn_half(half, se_rows_ln2, publish=False, carry=None):
            base_c = half * NCHH
            if half == 0:
                for m2 in range(KD):
                    CAT2.dma(PO, out=wo_b[m2, :, :], in_=wo_d[m2, :, :])
                dv_xnh = attn_apply(0, se_rows_ln2)
                kvab = [kvaA, kvaB]
                SY.wait_ge(CAT.s.h, CAT.s.n)
                SY.wait_ge(sPE.h, sPE.n)
                ktok = [KSL[0].dma(SY, out=kvab[0][:, :], in_=wkva_b[0, :, :])]
                qtok = [QSL.dma(SY, out=qbuf[:, :], in_=wq_b[0, :, :])]
                gtok = [GSL.dma(SY, out=gbuf[:, :], in_=wg_b[0, :, :])]
                prev_op = sPO.n
            else:
                dv_xnh = carry["dv_xnh"]
                kvab = [kvaA, kvaB]
                ktok = carry["ktok"]
                qtok = carry["qtok"]
                gtok = carry["gtok"]
                prev_op = carry["op_tail"]

            TE.wait_ge(CP.s.h, consts_p)
            SEn.wait_ge(CW.s.h, consts_w)
            ssq_mm, q2b_dv = [], []
            u_done, aclip_done, w1_done, w2_done = [], [], [], []
            free_k, free_a, free_g, free_q, free_v = [], [], [], [], []
            for h in range(H):
                kb = kvab[h % 2]
                KSL[h % 2].wait(TE, ktok[h])
                if h == 0:
                    TE.wait_ge(sDV.h, dv_xnh[0])
                    TE.wait_ge(sPO.h, dv_xnh[1])
                    TE.wait_ge(sPO.h, prev_op)
                else:
                    TE.wait_ge(sSE.h, free_k[h - 1])
                mm = None
                for k in range(KD):
                    mm = TE.matmul(PB[:, 0:512], kb[:, k * 128:k * 128 + 128],
                                   xn[:, k * 512:k * 512 + 512], start=(k == 0), stop=(k == KD - 1))
                kmm = sPE.inc(mm)
                if h > 0:
                    TE.wait_ge(sDV.h, free_v[h - 1])
                mm = None
                for k in range(KD):
                    mm = TE.matmul(PB[:, 512:1024], kb[:, 2048 + k * 128:2048 + k * 128 + 128],
                                   xn[:, k * 512:k * 512 + 512], start=(k == 0), stop=(k == KD - 1))
                vmm = sPE.inc(mm)
                if h > 0:
                    TE.wait_ge(sSE.h, free_a[h - 1])
                mm = None
                for k in range(KD):
                    mm = TE.matmul(PC[:, 0:512], kb[:, 4096 + k * 128:4096 + k * 128 + 128],
                                   xn[:, k * 512:k * 512 + 512], start=(k == 0), stop=(k == KD - 1))
                amm = sPE.inc(mm)
                if h + 1 < H:
                    SY.wait_ge(sPE.h, amm)
                    ktok.append(KSL[(h + 1) % 2].dma(SY, out=kvab[(h + 1) % 2][:, :],
                                                     in_=wkva_b[h + 1, :, :]))
                QSL.wait(TE, qtok[h])
                if h > 0:
                    TE.wait_ge(sDV.h, free_q[h - 1])
                mm = None
                for k in range(KD):
                    mm = TE.matmul(PA[:, 0:512], qbuf[:, k * 128:k * 128 + 128],
                                   xn[:, k * 512:k * 512 + 512], start=(k == 0), stop=(k == KD - 1))
                qmm = sPE.inc(mm)
                GSL.wait(TE, gtok[h])
                if h > 0:
                    TE.wait_ge(sSE.h, free_g[h - 1])
                mm = None
                for k in range(KD):
                    mm = TE.matmul(PA[:, 512:1024], gbuf[:, k * 128:k * 128 + 128],
                                   xn[:, k * 512:k * 512 + 512], start=(k == 0), stop=(k == KD - 1))
                gmm = sPE.inc(mm)
                if h > 0:
                    TE.wait_ge(sDV.h, q2b_dv[h - 1])
                    mm = TE.matmul(PD[0:16, 512:1024], ind16[:, (h - 1) * 16:h * 16], E5[:, :],
                                   start=(h - 1 == 0), stop=False)
                    ssq_mm.append(sPE.inc(mm))
                if h + 1 < H:
                    SY.wait_ge(sPE.h, qmm)
                    qtok.append(QSL.dma(SY, out=qbuf[:, :], in_=wq_b[h + 1, :, :]))
                    SY.wait_ge(sPE.h, gmm)
                    gtok.append(GSL.dma(SY, out=gbuf[:, :], in_=wg_b[h + 1, :, :]))
                SEn.wait_ge(sPE.h, kmm)
                if h == 0:
                    SEn.wait_ge(sDV.h, dv_xnh[0])
                else:
                    SEn.wait_ge(sDV.h, u_done[h - 1])
                tk = SEn.activation(E1[:, :], PB[:, 0:512], AF.Tanh)
                free_k.append(sSE.n)
                SEn.wait_ge(sPE.h, amm)
                if h == 0:
                    SEn.wait_ge(sPO.h, sPO.n)
                asig = SEn.activation(a_f32[:, h * 512:h * 512 + 512], PC[:, 0:512],
                                      AF.Sigmoid, bias=b2col[:, 0:1])
                free_a.append(sSE.n)
                SEn.wait_ge(sPE.h, gmm)
                if h == 0:
                    OSL[0].wait(SEn, OSL[0].s.n)
                    OSL[1].wait(SEn, OSL[1].s.n)
                    SEn.wait_ge(sPO.h, sPO.n)
                else:
                    SEn.wait_ge(sDV.h, w1_done[h - 1])
                gse = SEn.activation(E2[:, :], PA[:, 512:1024], AF.Sigmoid)
                if h > 0:
                    SEn.wait_ge(sDV.h, w2_done[h - 1])
                g2se = SEn.activation(E4[:, :], PA[:, 512:1024], AF.Sigmoid, scale=-1.0)
                free_g.append(sSE.n)
                VE.wait_ge(sPE.h, qmm)
                if h > 0:
                    VE.wait_ge(sDV.h, q2b_dv[h - 1])
                    VE.wait_ge(sPE.h, ssq_mm[h - 1])
                qcp = VE.tensor_copy(E3[:, :], PA[:, 0:512])
                free_q.append(sDV.n)
                VE.wait_ge(sSE.h, tk)
                VE.wait_ge(sPE.h, vmm)
                u_done.append(VE.tensor_tensor(
                    u_bf[:, h * 512:h * 512 + 512], E1[:, :], PB[:, 512:1024], OP.mult))
                VE.wait_ge(sSE.h, g2se)
                w2_done.append(VE.tensor_tensor(
                    w2_bf[:, h * 512:h * 512 + 512], E4[:, :], PB[:, 512:1024], OP.mult))
                free_v.append(sDV.n)
                VE.wait_ge(sSE.h, gse)
                w1_done.append(VE.tensor_tensor(
                    w1_bf[:, h * 512:h * 512 + 512], E2[:, :], E3[:, :], OP.mult))
                q2b_dv.append(VE.tensor_tensor(E5[:, :], E3[:, :], E3[:, :], OP.mult))
                VE.wait_ge(sSE.h, asig)
                aclip_done.append(VE.tensor_scalar(
                    a_f32[:, h * 512:h * 512 + 512], a_f32[:, h * 512:h * 512 + 512],
                    0.9995, 0.6, OP.min, OP.max))
            TE.wait_ge(sDV.h, q2b_dv[H - 1])
            mm = TE.matmul(PD[0:16, 512:1024], ind16[:, (H - 1) * 16:H * 16], E5[:, :],
                           start=False, stop=True)
            ssq_last = sPE.inc(mm)

            VE.wait_ge(sPE.h, ssq_last)
            VE.tensor_copy(RQ16[:, :], PD[0:16, 512:1024])
            VE.tensor_scalar(RQ16[:, :], RQ16[:, :], 1.0 / 128, RMSEPS, OP.mult, OP.add)
            SEn.wait_ge(sDV.h, sDV.n)
            SEn.activation(S3[0:16, 0:512], RQ16[:, :], AF.Ln)
            SEn.activation(RQ16[:, :], S3[0:16, 0:512], AF.Exp, scale=-0.5)
            VE.wait_ge(sSE.h, sSE.n)
            rqb_dv = VE.tensor_copy(RQ16b[:, :], RQ16[:, :])
            bcast_mm, w1rq = [], []
            for h in range(H):
                if h == 0:
                    TE.wait_ge(sDV.h, rqb_dv)
                if h >= 2:
                    TE.wait_ge(sDV.h, w1rq[h - 2])
                mm = TE.matmul(PA[:, (h % 2) * 512:(h % 2) * 512 + 512],
                               indr[:, h * 128:h * 128 + 128], RQ16b[:, :], start=True, stop=True)
                bcast_mm.append(sPE.inc(mm))
                VE.wait_ge(sPE.h, bcast_mm[h])
                w1rq.append(VE.tensor_tensor(
                    w1_bf[:, h * 512:h * 512 + 512], w1_bf[:, h * 512:h * 512 + 512],
                    PA[:, (h % 2) * 512:(h % 2) * 512 + 512], OP.mult))

            out = {}
            if half == 0:
                out["dv_xnh"] = attn_apply(1, se_rows_ln2)
                SY.wait_ge(sPE.h, sPE.n)
                out["ktok"] = [KSL[0].dma(SY, out=kvaA[:, :], in_=wkva_b[0, :, :])]
            S2r = S2[:, :].rearrange("p (h t) -> p h t", h=H)
            S3r = S3[:, :].rearrange("p (h t) -> p h t", h=H)
            S4r = S4[:, :].rearrange("p (h t) -> p h t", h=H)
            a3 = a_f32.rearrange("p (h t) -> p h t", h=H)
            u3 = u_bf.rearrange("p (h t) -> p h t", h=H)
            w13 = w1_bf.rearrange("p (h t) -> p h t", h=H)
            w23 = w2_bf.rearrange("p (h t) -> p h t", h=H)
            y3 = y_bf.rearrange("p (h t) -> p h t", h=H)
            R3 = R_bf.rearrange("p (h t) -> p h t", h=H)
            scol_edge = sDV.n
            last_comb = None
            op_tail = None
            pool_y = [0] * NCHH
            pool_pf = [0] * NCHH
            SEn.wait_ge(sPE.h, sPE.n)
            SEn.wait_ge(CAT2.s.h, CAT2.s.n)
            wtk = [wosl[0].dma(SEn, out=wob[0][:, :], in_=wo_b[0, :, :]),
                   wosl[1].dma(SEn, out=wob[1][:, :], in_=wo_b[1, :, :])]
            RV = nc.vector
            for cc in range(NCHH):
                c = base_c + cc
                Sp = [S1, S5][cc % 2]
                Spr = Sp[:, :].rearrange("p (h t) -> p h t", h=H)
                if cc == 0:
                    VE.wait_ge(sDV.h, w1rq[-1])
                    VE.wait_ge(OSL[0].s.h, OSL[0].s.n)
                    VE.wait_ge(OSL[1].s.h, OSL[1].s.n)
                    VE.wait_ge(sPO.h, sPO.n)
                if cc >= 2 and (base_c + cc - 2) < NCORR:
                    RV.wait_ge(sPO.h, pool_pf[cc - 2])
                for hh in range(H):
                    o = hh * 512 + cc * 64
                    ins = RV.tensor_tensor_scan(Sp[:, hh * 64:hh * 64 + 64],
                                                a_f32[:, o:o + 64], a_f32[:, o:o + 64],
                                                1.0, OP.mult, OP.bypass)
                    if hh == H - 1:
                        sDV.inc(ins)
                VE.tensor_scalar(S3[:, :], Sp[:, :], 1e6, 1.0, OP.mult, OP.min)
                t_tok = VE.tensor_tensor(S3r[:, :, :], u3[:, :, cc * 64:cc * 64 + 64],
                                         S3r[:, :, :], OP.mult)
                Std = [S4, S2][cc % 2]
                Stdr = [S4r, S2r][cc % 2]
                RV.wait_ge(sDV.h, t_tok)
                RV.wait_ge(sDV.h, scol_edge)
                if cc > 1:
                    RV.wait_ge(sPO.h, pool_y[cc - 2])
                for hh in range(H):
                    o = hh * 512 + cc * 64
                    ins = RV.tensor_tensor_scan(Std[:, hh * 64:hh * 64 + 64],
                                                a_f32[:, o:o + 64], S3[:, hh * 64:hh * 64 + 64],
                                                scol[:, hh:hh + 1], OP.mult, OP.add)
                    if hh == H - 1:
                        sDV.inc(ins)
                Acur = [AcA, AcB2][c % 2]
                Anxt = [AcA, AcB2][(c + 1) % 2]
                scol_edge = VE.tensor_copy(scol[:, :].unsqueeze(2), Stdr[:, :, 63:64])
                if cc >= 1 and (c - 1) < NCORR:
                    VE.wait_ge(sPO.h, pool_pf[cc - 1])
                VE.tensor_tensor(Anxt[:, :].unsqueeze(2), Acur[:, :].unsqueeze(2),
                                 Spr[:, :, 63:64], OP.mult)
                PL.wait_ge(sDV.h, scol_edge)
                if c < NCORR:
                    AcBc = Acur[:, :].unsqueeze(2).broadcast_to([128, H, 64])
                    a3c = a3[:, :, cc * 64:cc * 64 + 64]
                    PL.tensor_tensor(a3c, Spr[:, :, :], AcBc, OP.mult)
                    pool_pf[cc] = PL.tensor_tensor(R3[:, :, c * 64:c * 64 + 64],
                                                   w13[:, :, cc * 64:cc * 64 + 64], a3c, OP.mult)
                PL.tensor_tensor(y3[:, :, cc * 64:cc * 64 + 64],
                                 w13[:, :, cc * 64:cc * 64 + 64], Stdr[:, :, :], OP.mult)
                last_comb = pool_y[cc] = PL.tensor_tensor(
                    y3[:, :, cc * 64:cc * 64 + 64], y3[:, :, cc * 64:cc * 64 + 64],
                    w23[:, :, cc * 64:cc * 64 + 64], OP.add)
                if cc == NCHH - 1 and publish:
                    PO.wait_ge(sDV.h, scol_edge + 1)
                    ccA = cc_in.ap()[0:1, :].rearrange("r (d h) -> (r d) h", d=128)
                    ccB = cc_in.ap()[1:2, :].rearrange("r (d h) -> (r d) h", d=128)
                    CP.dma(PO, out=ccA, in_=AcA[:, :])
                    pubtok = CP.dma(PO, out=ccB, in_=scol[:, :])
                    PO.wait_ge(CP.s.h, pubtok)
                    PO.collective_compute(
                        "AllGather", OP.bypass, replica_groups=[list(range(NCORES))],
                        ins=[cc_in.ap().opt()], outs=[cc_out.ap().opt()],
                    ).then_inc(sCC.h)
                    sCC.n += 1
                if cc == 3 or cc == NCHH - 1:
                    op_tail = outproj_group(half, cc // 4, last_comb, wtk, W=256)
                    if cc < NCHH - 1:
                        SEn.wait_ge(sPE.h, sPE.n)
                        wtk = [wosl[0].dma(SEn, out=wob[0][:, :], in_=wo_b[0, :, :]),
                               wosl[1].dma(SEn, out=wob[1][:, :], in_=wo_b[1, :, :])]
            if half == 0:
                SY.wait_ge(sPE.h, sPE.n)
                out["qtok"] = [QSL.dma(SY, out=qbuf[:, :], in_=wq_b[0, :, :])]
                out["gtok"] = [GSL.dma(SY, out=gbuf[:, :], in_=wg_b[0, :, :])]
            out["op_tail"] = op_tail
            return out

        if STAGE >= 1:
            wtok1 = [WSL[0].dma(PO, out=WB[:, 0:4096], in_=wgu[0, :, :]),
                     WSL[1].dma(PO, out=WB[:, 4096:8192], in_=wgu[1, :, :])]
        pe_ln1 = ln_stats_full(x_in)
        se_rows = ln_rows(pe_ln1)
        dv_bc = ln_bcast(se_rows)
        dv_xn = ln_apply_full(x_in, dv_bc)
        if STAGE == 0:
            PO.wait_ge(sDV.h, dv_xn)
            t1 = None
            for g in range(KD):
                t1 = CP.dma(PO, out=out_d[g * 128:(g + 1) * 128, :],
                            in_=xn[:, g * 1024:(g + 1) * 1024])
            PO.wait_ge(CP.s.h, t1)
            SEn.wait_ge(CP.s.h, t1)
        if STAGE >= 1:
            ffn1_dst = out_d if STAGE == 1 else x1sp
            cast_jobs = []
            if STAGE >= 2:
                big = []
                for h in range(H):
                    big.append((wkva_b[h, :, 0:3072], wkva[h, :, 0:3072]))
                    big.append((wkva_b[h, :, 3072:6144], wkva[h, :, 3072:6144]))
                sml = []
                for h in range(H):
                    sml.append((wq_b[h, :, :], wq_d[h, :, :]))
                    sml.append((wg_b[h, :, :], wg_d[h, :, :]))
                while big or sml:
                    if big:
                        cast_jobs.append(big.pop(0))
                    if sml:
                        cast_jobs.append(sml.pop(0))
            pe_ln2 = ffn_phase(x_in, ffn1_dst, dv_xn, fuse_ln=(STAGE >= 2), wtok=wtok1,
                               casts=cast_jobs)
        if STAGE >= 2:
            se_rows2 = ln_rows(pe_ln2)
            r0 = attn_half(0, se_rows2)
            attn_half(1, se_rows2, publish=True, carry=r0)
            PO.wait_ge(sCC.h, sCC.n)
            gv = cc_out.ap().rearrange("r (d h) -> r d h", d=128).transpose([1, 0, 2])
            gtok = CP.dma(PO, out=gab[:, :].rearrange("p (r h) -> p r h", r=2 * NCORES), in_=gv)
            CP.wait(VE, gtok)
            for j in range(7):
                VE.tensor_scalar(T1[:, :], gab[:, 2 * j * 16:(2 * j + 1) * 16],
                                 mskT[:, j:j + 1], 0.0, OP.mult, OP.add)
                VE.tensor_scalar(T1[:, :], T1[:, :], mskT[:, 8 + j:9 + j], 0.0, OP.add, OP.add)
                VE.tensor_tensor(sin_t[:, :], sin_t[:, :], T1[:, :], OP.mult)
                VE.tensor_scalar(T2[:, :], gab[:, (2 * j + 1) * 16:(2 * j + 2) * 16],
                                 mskT[:, j:j + 1], 0.0, OP.mult, OP.add)
                VE.tensor_tensor(sin_t[:, :], sin_t[:, :], T2[:, :], OP.add)
            VE.wait_ge(sPO.h, sPO.n)
            sinB = sin_t[:, :].unsqueeze(2).broadcast_to([128, H, NCORR * 64])
            R3f = R_bf.rearrange("p (h t) -> p h t", h=H)
            c3 = corr_bf.rearrange("p (h t) -> p h t", h=H)
            corr_dv = (VE.tensor_tensor(c3[:, :, :], R3f[:, :, :], sinB, OP.mult))
            SEn.wait_ge(sPE.h, sPE.n)
            wob2 = [WB[:, 0:2048], WB[:, 2048:4096]]
            wtk2 = [wosl[0].dma(SEn, out=wob2[0][:, :], in_=wo_b[0, :, :]),
                    wosl[1].dma(SEn, out=wob2[1][:, :], in_=wo_b[1, :, :])]
            SY.wait_ge(sDV.h, sDV.n)
            xtk2 = [STS[0].dma(SY, out=st_bufs[0][:, :], in_=x2sp[0:128, :]),
                    STS[1].dma(SY, out=st_bufs[1][:, :], in_=x2sp[128:256, :])]
            op2_mm, op2_evac, op2_st, stat_mm, xn_cast = [], [], [], [], []
            for m2 in range(KD):
                pd_ = PB[:, 0:256] if (m2 % 2 == 0) else PB[:, 512:768]
                wosl[m2 % 2].wait(TE, wtk2[m2])
                if m2 == 0:
                    TE.wait_ge(sDV.h, corr_dv)
                if m2 >= 2:
                    TE.wait_ge(sDV.h, op2_evac[m2 - 2])
                mm = None
                for k in range(KD):
                    mm = TE.matmul(pd_, wob2[m2 % 2][:, k * 128:k * 128 + 128],
                                   corr_bf[:, k * 256:k * 256 + 256], start=(k == 0), stop=(k == KD - 1))
                op2_mm.append(sPE.inc(mm))
                if m2 + 2 < KD:
                    SEn.wait_ge(sPE.h, op2_mm[m2])
                    wtk2.append(wosl[m2 % 2].dma(SEn, out=wob2[m2 % 2][:, :], in_=wo_b[m2 + 2, :, :]))
                ob = [S4, S5][m2 % 2]
                VE.wait_ge(sPE.h, op2_mm[m2])
                STS[m2 % 2].wait(VE, xtk2[m2])
                OSL[m2 % 2].wait(VE, OSL[m2 % 2].s.n)
                op2_evac.append((VE.tensor_tensor(
                    ob[:, 0:256], pd_, st_bufs[m2 % 2][:, 0:256], OP.add)))
                if m2 > 0:
                    SEn.wait_ge(sPE.h, stat_mm[m2 - 1])
                    SEn.wait_ge(sPO.h, xn_cast[m2 - 1])
                SEn.wait_ge(sDV.h, op2_evac[m2])
                SEn.copy(l1v[:, 0:256], ob[:, 0:256])
                SEn.copy(l1v[:, 256:1024], st_bufs[m2 % 2][:, 256:1024])
                SEn.activation(l2v[:, 0:256], ob[:, 0:256], AF.Square)
                sqt = SEn.activation(l2v[:, 256:1024], st_bufs[m2 % 2][:, 256:1024], AF.Square)
                TE.wait_ge(sSE.h, sqt)
                mm = None
                for c0 in (0, 512):
                    TE.matmul(PC[0:1, c0:c0 + 512], ones_col[:, :], l1v[:, c0:c0 + 512],
                              start=(m2 == 0), stop=(m2 == KD - 1))
                    mm = TE.matmul(PD[0:1, c0:c0 + 512], ones_col[:, :], l2v[:, c0:c0 + 512],
                                   start=(m2 == 0), stop=(m2 == KD - 1))
                stat_mm.append(sPE.inc(mm))
                if m2 + 2 < KD:
                    SY.wait_ge(sDV.h, op2_evac[m2])
                    SY.wait_ge(sSE.h, sqt)
                    xtk2.append(STS[m2 % 2].dma(
                        SY, out=st_bufs[m2 % 2][:, :],
                        in_=x2sp[(m2 + 2) * 128:(m2 + 3) * 128, :]))
                PL.wait_ge(sSE.h, sqt)
                PL.wait_ge(sPE.h, sPE.n)
                xn_cast.append(PL.tensor_copy(xn[:, m2 * 1024:(m2 + 1) * 1024], l1v[:, :]))
                SEn.wait_ge(sDV.h, op2_evac[m2])
                op2_st.append(OSL[m2 % 2].dma(
                    SEn, out=x2sp[m2 * 128:(m2 + 1) * 128, 0:256], in_=ob[:, 0:256]))
        if STAGE == 2:
            SY.wait_ge(OSL[0].s.h, OSL[0].s.n)
            SY.wait_ge(OSL[1].s.h, OSL[1].s.n)
            SY.wait_ge(sDV.h, sDV.n)
            btoks = []
            for g in range(KD):
                if g >= 2:
                    OSL[g % 2].wait(SY, btoks[g - 2][1])
                lt = STS[g % 2].dma(SY, out=st_bufs[g % 2][:, :],
                                    in_=x2sp[g * 128:(g + 1) * 128, :])
                STS[g % 2].wait(SEn, lt)
                stt = OSL[g % 2].dma(SEn, out=out_d[g * 128:(g + 1) * 128, :],
                                     in_=st_bufs[g % 2][:, :])
                btoks.append((lt, stt))
        if STAGE >= 3:
            PO.wait_ge(sPE.h, sPE.n)
            wtok2 = [WSL[0].dma(PO, out=WB[:, 0:4096], in_=wgu[0, :, :]),
                     WSL[1].dma(PO, out=WB[:, 4096:8192], in_=wgu[1, :, :])]
            se_rows3 = ln_rows(stat_mm[-1])
            dv_bc3 = ln_bcast(se_rows3)
            dv_xn3 = ln_apply_full(x2sp, dv_bc3)
            ffn_phase(x2sp, out_d, dv_xn3, fuse_ln=False, wtok=wtok2)
        SEn.wait_ge(OSL[0].s.h, OSL[0].s.n)
        SEn.wait_ge(OSL[1].s.h, OSL[1].s.n)
    es.close()
    return nc


def _tiles(w):
    O, I = w.shape
    nm, nk = O // 128, I // 128
    t = w.reshape(nm, 128, nk, 128).transpose(0, 3, 2, 1)
    return np.ascontiguousarray(t).reshape(nm, 128, I)


def kernel(L0_attn_x, L0_attn_w_in, L0_attn_w_out, L0_ffn_gate_w, L0_ffn_up_w,
           L0_ffn_down_w, L0_ffn_gate_s, L0_ffn_up_s, L0_ffn_down_s):
    x = np.ascontiguousarray(np.asarray(L0_attn_x, dtype=np.float32))
    w_in = np.asarray(L0_attn_w_in, dtype=np.float32)
    w_out = np.asarray(L0_attn_w_out, dtype=np.float32)
    gate_w = np.asarray(L0_ffn_gate_w, dtype=np.float32)
    up_w = np.asarray(L0_ffn_up_w, dtype=np.float32)
    down_w = np.asarray(L0_ffn_down_w, dtype=np.float32)
    gate_s = np.asarray(L0_ffn_gate_s, dtype=np.float32)
    up_s = np.asarray(L0_ffn_up_s, dtype=np.float32)
    down_s = np.asarray(L0_ffn_down_s, dtype=np.float32)

    tg = _tiles(gate_w)
    tu = _tiles(up_w)
    wgu_h = np.ascontiguousarray(np.concatenate([tg, tu], axis=2))
    wd_h = _tiles(down_w)
    wit = _tiles(w_in)
    wkva_h = np.ascontiguousarray(np.concatenate([wit[16:32], wit[32:48], wit[48:64]], axis=2))
    wq_h = np.ascontiguousarray(wit[0:16])
    wg_h = np.ascontiguousarray(wit[64:80])
    wo_h = _tiles(w_out)
    gs2 = np.ascontiguousarray(gate_s.reshape(KI, 128).T)
    us2 = np.ascontiguousarray(up_s.reshape(KI, 128).T)
    ds2 = np.ascontiguousarray(down_s.reshape(KD, 128).T)
    ind16_h = np.zeros((128, 16 * H), dtype=np.float32)
    for h in range(H):
        ind16_h[:, h * 16 + h] = 1.0
    indr_h = np.zeros((16, 128 * H), dtype=np.float32)
    for h in range(H):
        indr_h[h, h * 128:(h + 1) * 128] = 1.0

    nc = build_program()
    in_maps = []
    for c in range(NCORES):
        b, q = c // 4, c % 4
        xs = np.ascontiguousarray(x[b, q * TPC:(q + 1) * TPC, :].T)
        msk = np.zeros((128, 16), dtype=np.float32)
        for j in range(8):
            beta = 1.0 if (j // 4 == b and j < c) else 0.0
            msk[:, j] = beta
            msk[:, 8 + j] = 1.0 - beta
        in_maps.append({
            "x": xs, "wgu": wgu_h, "wd": wd_h, "wkva": wkva_h, "wq": wq_h, "wgp": wg_h,
            "wo": wo_h, "gs": gs2, "us": us2, "ds": ds2, "msk": msk,
            "ind16": ind16_h, "indr": indr_h,
        })
    res = run_bass_kernel_spmd(nc, in_maps, core_ids=list(range(NCORES)))
    out = np.empty((B, S, D), dtype=np.float32)
    for c in range(NCORES):
        b, q = c // 4, c % 4
        out[b, q * TPC:(q + 1) * TPC, :] = res.results[c]["out"].T
    return out


# version 17
# speedup vs baseline: 1.0079x; 1.0079x over previous
import numpy as np
import concourse.bass as bass
import concourse.mybir as mybir
from concourse.bass_utils import run_bass_kernel_spmd
from contextlib import ExitStack

F32 = mybir.dt.float32
BF16 = mybir.dt.bfloat16
AF = mybir.ActivationFunctionType
OP = mybir.AluOpType

NCORES = 8
B, S, D = 2, 4096, 2048
TPC = 1024
HALF = 512
KD = D // 128
INTER = 5632
KI = INTER // 128
H = 16
CH = 64
NCH = TPC // CH
NCHH = HALF // CH
NCORR = 4
LNEPS = 1e-5
RMSEPS = 1.1920928955078125e-07
CLAMP = 1e-6

STAGE = 3


class Sem:
    def __init__(self, h):
        self.h = h
        self.n = 0

    def inc(self, instr, k=1):
        instr.then_inc(self.h, k)
        self.n += k
        return self.n


class DSlot:
    def __init__(self, sem):
        self.s = sem

    def dma(self, eng, **kw):
        return self.s.inc(eng.dma_start(**kw), 16)

    def wait(self, eng, tok):
        eng.wait_ge(self.s.h, tok)


def build_program():
    nc = bass.Bass("TRN2", target_bir_lowering=False, debug=False)
    x_in = nc.dram_tensor("x", [D, TPC], F32, kind="ExternalInput").ap()
    wgu = nc.dram_tensor("wgu", [KI, 128, 2 * D], F32, kind="ExternalInput").ap()
    wd = nc.dram_tensor("wd", [KD, 128, INTER], F32, kind="ExternalInput").ap()
    wkva = nc.dram_tensor("wkva", [H, 128, 3 * D], F32, kind="ExternalInput").ap()
    wq_d = nc.dram_tensor("wq", [H, 128, D], F32, kind="ExternalInput").ap()
    wg_d = nc.dram_tensor("wgp", [H, 128, D], F32, kind="ExternalInput").ap()
    wo_d = nc.dram_tensor("wo", [KD, 128, D], F32, kind="ExternalInput").ap()
    gs_d = nc.dram_tensor("gs", [128, KI], F32, kind="ExternalInput").ap()
    us_d = nc.dram_tensor("us", [128, KI], F32, kind="ExternalInput").ap()
    ds_d = nc.dram_tensor("ds", [128, KD], F32, kind="ExternalInput").ap()
    msk_d = nc.dram_tensor("msk", [128, 16], F32, kind="ExternalInput").ap()
    ind16_d = nc.dram_tensor("ind16", [128, 16 * H], F32, kind="ExternalInput").ap()
    indr_d = nc.dram_tensor("indr", [16, 128 * H], F32, kind="ExternalInput").ap()
    out_d = nc.dram_tensor("out", [D, TPC], F32, kind="ExternalOutput").ap()
    x1sp = nc.dram_tensor("x1sp", [D, TPC], F32).ap()
    x2sp = nc.dram_tensor("x2sp", [D, TPC], F32).ap()
    wkva_b = nc.dram_tensor("wkva_b", [H, 128, 3 * D], BF16).ap()
    wq_b = nc.dram_tensor("wq_b", [H, 128, D], BF16).ap()
    wg_b = nc.dram_tensor("wg_b", [H, 128, D], BF16).ap()
    wo_b = nc.dram_tensor("wo_b", [KD, 128, D], BF16).ap()
    cc_in = nc.dram_tensor("cc_in", [2, D], F32)
    cc_out = nc.dram_tensor("cc_out", [2 * NCORES, D], F32)

    es = ExitStack()
    sbt = lambda name, shape, dt: es.enter_context(nc.sbuf_tensor(name, shape, dt))
    pst = lambda name, shape: es.enter_context(nc.psum_tensor(name, shape, F32))
    mksem = lambda name: Sem(es.enter_context(nc.semaphore(name)))
    mkslot = lambda name: DSlot(mksem(name))

    xn = sbt("xn", [128, 16384], BF16)
    AR = sbt("AR", [128, 53248], BF16)
    S1 = sbt("S1", [128, 1024], F32)
    S2 = sbt("S2", [128, 1024], F32)
    S3 = sbt("S3", [128, 1024], F32)
    S4 = sbt("S4", [128, 1024], F32)
    S5 = sbt("S5", [128, 1024], F32)
    E1 = sbt("E1", [128, 512], F32)
    E2 = sbt("E2", [128, 512], F32)
    E3 = sbt("E3", [128, 512], F32)
    E4 = sbt("E4", [128, 512], F32)
    E5 = sbt("E5", [128, 512], BF16)
    ST1 = sbt("ST1", [128, 1024], F32)
    ST2 = sbt("ST2", [128, 1024], F32)
    WB = sbt("WB", [128, 8192], BF16)
    r_mean = sbt("r_mean", [1, 1024], F32)
    r_rstd = sbt("r_rstd", [1, 1024], F32)
    RQ16 = sbt("RQ16", [16, 512], F32)
    RQ16b = sbt("RQ16b", [16, 512], BF16)
    ones_col = sbt("ones_col", [128, 1], BF16)
    ones_row = sbt("ones_row", [1, 128], F32)
    b2col = sbt("b2col", [128, 1], F32)
    epscol = sbt("epscol", [128, 1], F32)
    gsT = sbt("gsT", [128, KI], F32)
    usT = sbt("usT", [128, KI], F32)
    dsT = sbt("dsT", [128, KD], F32)
    mskT = sbt("mskT", [128, 16], F32)
    ind16 = sbt("ind16s", [128, 16 * H], BF16)
    indr = sbt("indrs", [16, 128 * H], BF16)
    AcA = sbt("AcA", [128, 16], F32)
    AcB2 = sbt("AcB2", [128, 16], F32)
    scol = sbt("scol", [128, 16], F32)
    sin_t = sbt("sin", [128, 16], F32)
    gab = sbt("gab", [128, 256], F32)
    T1 = sbt("T1", [128, 16], F32)
    T2 = sbt("T2", [128, 16], F32)

    PA = pst("PA", [128, 1024])
    PB = pst("PB", [128, 1024])
    PC = pst("PC", [128, 1024])
    PD = pst("PD", [128, 1024])

    sPE = mksem("sPE")
    sDV = mksem("sDV")
    sSE = mksem("sSE")
    sPO = mksem("sPO")
    sCC = mksem("sCC")
    STS = [mkslot("sST0"), mkslot("sST1")]
    WSL = [mkslot("sWA"), mkslot("sWB")]
    DSL = [mkslot("sDA"), mkslot("sDB")]
    KSL = [mkslot("sKA"), mkslot("sKB")]
    QSL = mkslot("sQ")
    GSL = mkslot("sG")
    OSL = [mkslot("sOA"), mkslot("sOB")]
    CW = mkslot("sCW")
    CP = mkslot("sCP")
    CAT = mkslot("sCT")
    CAT2 = mkslot("sCT2")
    WOS = [mkslot("sWO0"), mkslot("sWO1")]

    TE, PO, SY = nc.tensor, nc.gpsimd, nc.sync

    class OEng:
        def __init__(self, eng, sem):
            self._e = eng
            self._s = sem

        def wait_ge(self, h, v):
            self._e.wait_ge(h, v)

        def dma_start(self, **kw):
            return self._e.dma_start(**kw)

        def __getattr__(self, name):
            f = getattr(self._e, name)

            def wrapped(*a, **k):
                self._e.wait_ge(self._s.h, self._s.n)
                return self._s.inc(f(*a, **k))
            return wrapped

    VE = OEng(nc.vector, sDV)
    SEn = OEng(nc.scalar, sSE)
    PL = OEng(nc.gpsimd, sPO)

    a_f32 = AR[:, 0:16384].bitcast(F32)
    u_bf = AR[:, 16384:24576]
    w1_bf = AR[:, 24576:32768]
    w2_bf = AR[:, 32768:40960]
    y_bf = AR[:, 40960:49152]
    R_bf = AR[:, 49152:53248]
    hdn = AR[:, 0:KI * 1024]
    corr_bf = u_bf[:, 0:4096]
    l1v = AR[:, 45056:46080]
    l2v = AR[:, 46080:47104]
    kvaA = WB[:, 0:6144]
    kvaB = xn[:, 8192:14336]
    qbuf = WB[:, 6144:8192]
    gbuf = xn[:, 14336:16384]
    dwb = [xn[:, 0:5632], xn[:, 5632:11264]]
    wob = [qbuf, gbuf]
    wosl = WOS
    st_bufs = [ST1, ST2]

    with nc.Block() as blk:
        VE.memset(ones_col[:, :], 1.0)
        VE.memset(ones_row[:, :], 1.0)
        VE.memset(b2col[:, :], 2.0)
        VE.memset(epscol[:, :], LNEPS)
        VE.memset(scol[:, :], 0.0)
        VE.memset(AcA[:, :], 1.0)
        VE.memset(AcB2[:, :], 1.0)
        (VE.memset(sin_t[:, :], 0.0))
        CW.dma(SY, out=gsT[:, :], in_=gs_d)
        CW.dma(SY, out=usT[:, :], in_=us_d)
        CW.dma(SY, out=dsT[:, :], in_=ds_d)
        consts_w = CW.dma(SY, out=mskT[:, :], in_=msk_d)
        CP.dma(PO, out=ind16[:, :], in_=ind16_d)
        consts_p = CP.dma(PO, out=indr[:, :], in_=indr_d)

        def ln_stats_full(src_dram):
            SY.wait_ge(sDV.h, sDV.n)
            VE.wait_ge(sPE.h, sPE.n)
            toks = [STS[0].dma(SY, out=st_bufs[0][:, :], in_=src_dram[0:128, :]),
                    STS[1].dma(SY, out=st_bufs[1][:, :], in_=src_dram[128:256, :])]
            casts = []
            smms = []
            for g in range(KD):
                STS[g % 2].wait(VE, toks[g])
                if g > 0:
                    VE.wait_ge(sPE.h, smms[g - 1])
                cst = VE.tensor_copy(xn[:, g * 1024:(g + 1) * 1024], st_bufs[g % 2][:, :])
                if g > 0:
                    SEn.wait_ge(sPE.h, smms[g - 1])
                STS[g % 2].wait(SEn, toks[g])
                casts.append(SEn.activation(l2v[:, :], st_bufs[g % 2][:, :], AF.Square))
                TE.wait_ge(sDV.h, cst)
                TE.wait_ge(sSE.h, casts[g])
                mm = None
                for c0 in (0, 512):
                    TE.matmul(PC[0:1, c0:c0 + 512], ones_col[:, :],
                              xn[:, g * 1024 + c0:g * 1024 + c0 + 512],
                              start=(g == 0), stop=(g == KD - 1))
                    mm = TE.matmul(PD[0:1, c0:c0 + 512], ones_col[:, :], l2v[:, c0:c0 + 512],
                                   start=(g == 0), stop=(g == KD - 1))
                smms.append(sPE.inc(mm))
                if g + 2 < KD:
                    SY.wait_ge(sDV.h, cst)
                    SY.wait_ge(sSE.h, casts[g])
                    toks.append(STS[g % 2].dma(
                        SY, out=st_bufs[g % 2][:, :],
                        in_=src_dram[(g + 2) * 128:(g + 3) * 128, :]))
            return smms[-1]

        def ln_rows(pe_done):
            VE.wait_ge(sPE.h, pe_done)
            VE.wait_ge(OSL[0].s.h, OSL[0].s.n)
            VE.wait_ge(OSL[1].s.h, OSL[1].s.n)
            VE.tensor_scalar(r_mean[:, :], PC[0:1, :], 1.0 / D, 0.0, OP.mult, OP.add)
            VE.tensor_scalar(S3[0:1, :], PD[0:1, :], 1.0 / D, 0.0, OP.mult, OP.add)
            VE.tensor_tensor(S4[0:1, :], r_mean[:, :], r_mean[:, :], OP.mult)
            (VE.tensor_tensor(S3[0:1, :], S3[0:1, :], S4[0:1, :], OP.subtract))
            SEn.wait_ge(sDV.h, sDV.n)
            SEn.activation(S4[0:1, :], S3[0:1, :], AF.Ln, bias=epscol[0:1, 0:1])
            return (SEn.activation(r_rstd[:, :], S4[0:1, :], AF.Exp, scale=-0.5))

        def ln_bcast(se_rows, c0=0, n=1024, guard_dv=None):
            TE.wait_ge(sSE.h, se_rows)
            if guard_dv is not None:
                TE.wait_ge(sDV.h, guard_dv)
            mm = None
            for cc in range(0, n, 512):
                TE.matmul(PC[:, cc:cc + 512], ones_row[:, :], r_mean[0:1, c0 + cc:c0 + cc + 512],
                          start=True, stop=True)
                mm = TE.matmul(PD[:, cc:cc + 512], ones_row[:, :], r_rstd[0:1, c0 + cc:c0 + cc + 512],
                               start=True, stop=True)
            pe = sPE.inc(mm)
            VE.wait_ge(sPE.h, pe)
            VE.tensor_copy(S1[:, 0:n], PC[:, 0:n])
            return (VE.tensor_copy(S2[:, 0:n], PD[:, 0:n]))

        def ln_apply_full(src_dram, dv_bc):
            dv_tok = po_tok = None
            for g in range(KD):
                if g == 0:
                    VE.wait_ge(sDV.h, dv_bc)
                    PL.wait_ge(sDV.h, dv_bc)
                    PL.wait_ge(sPE.h, sPE.n)
                if g % 2 == 0:
                    VE.tensor_tensor(S3[:, :], xn[:, g * 1024:(g + 1) * 1024], S1[:, :], OP.subtract)
                    dv_tok = VE.tensor_tensor(
                        xn[:, g * 1024:(g + 1) * 1024], S3[:, :], S2[:, :], OP.mult)
                else:
                    PL.tensor_tensor(S4[:, :], xn[:, g * 1024:(g + 1) * 1024], S1[:, :], OP.subtract)
                    po_tok = PL.tensor_tensor(
                        xn[:, g * 1024:(g + 1) * 1024], S4[:, :], S2[:, :], OP.mult)
            return (dv_tok, po_tok)

        def ffn_phase(xsrc_dram, xdst_dram, dv_xn, fuse_ln, wtok=None, casts=None):
            casts = list(casts or [])
            SEn.wait_ge(CW.s.h, consts_w)
            VE.wait_ge(CW.s.h, consts_w)
            if wtok is None:
                PO.wait_ge(sPE.h, sPE.n)
                wtok = [WSL[0].dma(PO, out=WB[:, 0:4096], in_=wgu[0, :, :]),
                        WSL[1].dma(PO, out=WB[:, 4096:8192], in_=wgu[1, :, :])]
            gu_mm, gu_evac = [], []
            for m in range(KI):
                pg = PA if (m % 2 == 0) else PC
                pu = PB if (m % 2 == 0) else PD
                wbuf = WB[:, (m % 2) * 4096:(m % 2) * 4096 + 4096]
                WSL[m % 2].wait(TE, wtok[m])
                if m == 0:
                    TE.wait_ge(sDV.h, dv_xn[0])
                    TE.wait_ge(sPO.h, dv_xn[1])
                if m >= 2:
                    TE.wait_ge(sDV.h, gu_evac[m - 2])
                mm = None
                for k in range(KD):
                    for cc in (0, 512):
                        TE.matmul(pg[:, cc:cc + 512], wbuf[:, k * 128:k * 128 + 128],
                                  xn[:, k * 1024 + cc:k * 1024 + cc + 512],
                                  start=(k == 0), stop=(k == KD - 1))
                for k in range(KD):
                    for cc in (0, 512):
                        mm = TE.matmul(pu[:, cc:cc + 512], wbuf[:, 2048 + k * 128:2048 + k * 128 + 128],
                                       xn[:, k * 1024 + cc:k * 1024 + cc + 512],
                                       start=(k == 0), stop=(k == KD - 1))
                gu_mm.append(sPE.inc(mm))
                if m + 2 < KI:
                    PO.wait_ge(sPE.h, gu_mm[m])
                    wtok.append(WSL[m % 2].dma(
                        PO, out=WB[:, (m % 2) * 4096:(m % 2) * 4096 + 4096],
                        in_=wgu[m + 2, :, :]))
                if casts:
                    dst, src = casts.pop(0)
                    CAT.dma(PO, out=dst, in_=src)
                sg = [S4, S5][m % 2]
                SEn.wait_ge(sPE.h, gu_mm[m])
                if m >= 2:
                    SEn.wait_ge(sDV.h, gu_evac[m - 2])
                sse_m = (SEn.activation(sg[:, :], pg[:, :], AF.Silu, scale=gsT[:, m:m + 1]))
                VE.wait_ge(sPE.h, gu_mm[m])
                VE.tensor_scalar(S3[:, :], pu[:, :], usT[:, m:m + 1], 0.0, OP.mult, OP.add)
                VE.wait_ge(sSE.h, sse_m)
                gu_evac.append((VE.tensor_tensor(
                    hdn[:, m * 1024:(m + 1) * 1024], sg[:, :], S3[:, :], OP.mult)))
            hdn_ready = gu_evac[-1]
            PO.wait_ge(sPE.h, gu_mm[KI - 1])
            dtok = [DSL[0].dma(PO, out=dwb[0][:, :], in_=wd[0, :, :]),
                    DSL[1].dma(PO, out=dwb[1][:, :], in_=wd[1, :, :])]
            SY.wait_ge(sDV.h, sDV.n)
            xtok = [STS[0].dma(SY, out=st_bufs[0][:, :], in_=xsrc_dram[0:128, :]),
                    STS[1].dma(SY, out=st_bufs[1][:, :], in_=xsrc_dram[128:256, :])]
            dn_mm, dn_evac, dn_st, fmms = [], [], [], []
            for m2 in range(KD):
                pd_ = PA if (m2 % 2 == 0) else PB
                DSL[m2 % 2].wait(TE, dtok[m2])
                if m2 == 0:
                    TE.wait_ge(sDV.h, hdn_ready)
                if m2 >= 2:
                    TE.wait_ge(sDV.h, dn_evac[m2 - 2])
                mm = None
                for k in range(KI):
                    for cc in (0, 512):
                        mm = TE.matmul(pd_[:, cc:cc + 512], dwb[m2 % 2][:, k * 128:k * 128 + 128],
                                       hdn[:, k * 1024 + cc:k * 1024 + cc + 512],
                                       start=(k == 0), stop=(k == KI - 1))
                dn_mm.append(sPE.inc(mm))
                if m2 + 2 < KD:
                    PO.wait_ge(sPE.h, dn_mm[m2])
                    dtok.append(DSL[m2 % 2].dma(PO, out=dwb[m2 % 2][:, :], in_=wd[m2 + 2, :, :]))
                for _ in range(2):
                    if casts:
                        dst, src = casts.pop(0)
                        CAT.dma(PO, out=dst, in_=src)
                ob = [S4, S5][m2 % 2]
                VE.wait_ge(sPE.h, dn_mm[m2])
                STS[m2 % 2].wait(VE, xtok[m2])
                OSL[m2 % 2].wait(VE, OSL[m2 % 2].s.n)
                VE.tensor_scalar(S3[:, :], pd_[:, :], dsT[:, m2:m2 + 1], 0.5, OP.mult, OP.mult)
                dn_evac.append((VE.tensor_tensor(
                    ob[:, :], S3[:, :], st_bufs[m2 % 2][:, :], OP.add)))
                if m2 + 2 < KD:
                    SY.wait_ge(sDV.h, dn_evac[m2])
                    xtok.append(STS[m2 % 2].dma(
                        SY, out=st_bufs[m2 % 2][:, :],
                        in_=xsrc_dram[(m2 + 2) * 128:(m2 + 3) * 128, :]))
                SEn.wait_ge(sDV.h, dn_evac[m2])
                dn_st.append(OSL[m2 % 2].dma(
                    SEn, out=xdst_dram[m2 * 128:(m2 + 1) * 128, :], in_=ob[:, :]))
                if fuse_ln:
                    if m2 > 0:
                        VE.wait_ge(sPE.h, fmms[m2 - 1])
                    VE.tensor_copy(l1v[:, :], ob[:, :])
                    cst = (VE.tensor_tensor(l2v[:, :], ob[:, :], ob[:, :], OP.mult))
                    TE.wait_ge(sDV.h, cst)
                    mm = None
                    for c0 in (0, 512):
                        TE.matmul(PC[0:1, c0:c0 + 512], ones_col[:, :], l1v[:, c0:c0 + 512],
                                  start=(m2 == 0), stop=(m2 == KD - 1))
                        mm = TE.matmul(PD[0:1, c0:c0 + 512], ones_col[:, :], l2v[:, c0:c0 + 512],
                                       start=(m2 == 0), stop=(m2 == KD - 1))
                    fmms.append(sPE.inc(mm))
            while casts:
                dst, src = casts.pop(0)
                CAT.dma(PO, out=dst, in_=src)
            return fmms[-1] if fuse_ln else None

        def attn_apply(half, se_rows_ln2):
            cofs = half * 512
            TE.wait_ge(sSE.h, se_rows_ln2)
            TE.wait_ge(sDV.h, sDV.n)
            TE.matmul(PD[:, 0:512], ones_row[:, :], r_mean[0:1, cofs:cofs + 512],
                      start=True, stop=True)
            mm = TE.matmul(PD[:, 512:1024], ones_row[:, :], r_rstd[0:1, cofs:cofs + 512],
                           start=True, stop=True)
            pe = sPE.inc(mm)
            SEn.wait_ge(sPE.h, pe)
            SEn.wait_ge(sDV.h, sDV.n)
            SEn.copy(E3[:, :], PD[:, 0:512])
            rbtok = SEn.copy(E4[:, :], PD[:, 512:1024])
            SY.wait_ge(sDV.h, sDV.n)
            SY.wait_ge(sPO.h, sPO.n)
            toks = [STS[0].dma(SY, out=st_bufs[0][:, 0:512], in_=x1sp[0:128, cofs:cofs + 512]),
                    STS[1].dma(SY, out=st_bufs[1][:, 0:512], in_=x1sp[128:256, cofs:cofs + 512])]
            dv_tok = po_tok = None
            ptmp = E2 if half == 0 else S1
            for g in range(KD):
                if g == 0:
                    VE.wait_ge(sSE.h, rbtok)
                    PL.wait_ge(sSE.h, rbtok)
                    if half == 0:
                        OSL[0].wait(PL, OSL[0].s.n)
                        OSL[1].wait(PL, OSL[1].s.n)
                if g % 2 == 0:
                    STS[0].wait(VE, toks[g])
                    VE.tensor_tensor(E1[:, :], st_bufs[0][:, 0:512], E3[:, :], OP.subtract)
                    dv_tok = VE.tensor_tensor(
                        xn[:, g * 512:g * 512 + 512], E1[:, :], E4[:, :], OP.mult)
                else:
                    STS[1].wait(PL, toks[g])
                    PL.tensor_tensor(ptmp[:, 0:512], st_bufs[1][:, 0:512], E3[:, :], OP.subtract)
                    po_tok = PL.tensor_tensor(
                        xn[:, g * 512:g * 512 + 512], ptmp[:, 0:512], E4[:, :], OP.mult)
                if g + 2 < KD:
                    if g % 2 == 0:
                        SY.wait_ge(sDV.h, dv_tok)
                    else:
                        SY.wait_ge(sPO.h, po_tok)
                    toks.append(STS[g % 2].dma(
                        SY, out=st_bufs[g % 2][:, 0:512],
                        in_=x1sp[(g + 2) * 128:(g + 3) * 128, cofs:cofs + 512]))
            return (dv_tok, po_tok)

        def outproj_group(half, grp, y_token, first_wtk, W=128):
            cofs2 = half * 512 + grp * W
            wtk = first_wtk
            SY.wait_ge(sDV.h, sDV.n)
            xtk = [STS[0].dma(SY, out=st_bufs[0][:, 0:W], in_=x1sp[0:128, cofs2:cofs2 + W]),
                   STS[1].dma(SY, out=st_bufs[1][:, 0:W], in_=x1sp[128:256, cofs2:cofs2 + W])]
            op_mm, op_cp, op_evac, op_st = [], [], [], []
            for m2 in range(KD):
                pd_ = PB if (m2 % 2 == 0) else PC
                wosl[m2 % 2].wait(TE, wtk[m2])
                if m2 == 0:
                    TE.wait_ge(sPO.h, y_token)
                if m2 >= 2:
                    TE.wait_ge(sSE.h, op_cp[m2 - 2])
                mm = None
                for k in range(KD):
                    mm = TE.matmul(pd_[:, 0:W], wob[m2 % 2][:, k * 128:k * 128 + 128],
                                   y_bf[:, k * 512 + grp * W:k * 512 + grp * W + W],
                                   start=(k == 0), stop=(k == KD - 1))
                op_mm.append(sPE.inc(mm))
                if m2 + 2 < KD:
                    SEn.wait_ge(sPE.h, op_mm[m2])
                    wtk.append(wosl[m2 % 2].dma(SEn, out=wob[m2 % 2][:, :],
                                                in_=wo_b[m2 + 2, :, :]))
                ob = E2[:, (m2 % 2) * 256:(m2 % 2) * 256 + W]
                SEn.wait_ge(sPE.h, op_mm[m2])
                OSL[m2 % 2].wait(SEn, OSL[m2 % 2].s.n)
                op_cp.append(SEn.copy(ob, pd_[:, 0:W]))
                PL.wait_ge(sSE.h, op_cp[m2])
                STS[m2 % 2].wait(PL, xtk[m2])
                OSL[m2 % 2].wait(PL, OSL[m2 % 2].s.n)
                op_evac.append(PL.tensor_tensor(ob, ob, st_bufs[m2 % 2][:, 0:W], OP.add))
                if m2 + 2 < KD:
                    SY.wait_ge(sPO.h, op_evac[m2])
                    xtk.append(STS[m2 % 2].dma(
                        SY, out=st_bufs[m2 % 2][:, 0:W],
                        in_=x1sp[(m2 + 2) * 128:(m2 + 3) * 128, cofs2:cofs2 + W]))
                SEn.wait_ge(sPO.h, op_evac[m2])
                op_st.append(OSL[m2 % 2].dma(
                    SEn, out=x2sp[m2 * 128:(m2 + 1) * 128, cofs2:cofs2 + W], in_=ob))
            return op_evac[-1]

        def attn_half(half, se_rows_ln2, publish=False, carry=None):
            base_c = half * NCHH
            if half == 0:
                for m2 in range(KD):
                    CAT2.dma(PO, out=wo_b[m2, :, :], in_=wo_d[m2, :, :])
                dv_xnh = attn_apply(0, se_rows_ln2)
                kvab = [kvaA, kvaB]
                SY.wait_ge(CAT.s.h, CAT.s.n)
                SY.wait_ge(sPE.h, sPE.n)
                ktok = [KSL[0].dma(SY, out=kvab[0][:, :], in_=wkva_b[0, :, :])]
                qtok = [QSL.dma(SY, out=qbuf[:, :], in_=wq_b[0, :, :])]
                gtok = [GSL.dma(SY, out=gbuf[:, :], in_=wg_b[0, :, :])]
                prev_op = sPO.n
            else:
                dv_xnh = carry["dv_xnh"]
                kvab = [kvaA, kvaB]
                ktok = carry["ktok"]
                qtok = carry["qtok"]
                gtok = carry["gtok"]
                prev_op = carry["op_tail"]

            TE.wait_ge(CP.s.h, consts_p)
            SEn.wait_ge(CW.s.h, consts_w)
            ssq_mm, q2b_dv = [], []
            u_done, aclip_done, w1_done, w2_done = [], [], [], []
            free_k, free_a, free_g, free_q, free_v = [], [], [], [], []
            for h in range(H):
                kb = kvab[h % 2]
                KSL[h % 2].wait(TE, ktok[h])
                if h == 0:
                    TE.wait_ge(sDV.h, dv_xnh[0])
                    TE.wait_ge(sPO.h, dv_xnh[1])
                    TE.wait_ge(sPO.h, prev_op)
                else:
                    TE.wait_ge(sSE.h, free_k[h - 1])
                mm = None
                for k in range(KD):
                    mm = TE.matmul(PB[:, 0:512], kb[:, k * 128:k * 128 + 128],
                                   xn[:, k * 512:k * 512 + 512], start=(k == 0), stop=(k == KD - 1))
                kmm = sPE.inc(mm)
                if h > 0:
                    TE.wait_ge(sDV.h, free_v[h - 1])
                mm = None
                for k in range(KD):
                    mm = TE.matmul(PB[:, 512:1024], kb[:, 2048 + k * 128:2048 + k * 128 + 128],
                                   xn[:, k * 512:k * 512 + 512], start=(k == 0), stop=(k == KD - 1))
                vmm = sPE.inc(mm)
                if h > 0:
                    TE.wait_ge(sSE.h, free_a[h - 1])
                mm = None
                for k in range(KD):
                    mm = TE.matmul(PC[:, 0:512], kb[:, 4096 + k * 128:4096 + k * 128 + 128],
                                   xn[:, k * 512:k * 512 + 512], start=(k == 0), stop=(k == KD - 1))
                amm = sPE.inc(mm)
                if h + 1 < H:
                    SY.wait_ge(sPE.h, amm)
                    ktok.append(KSL[(h + 1) % 2].dma(SY, out=kvab[(h + 1) % 2][:, :],
                                                     in_=wkva_b[h + 1, :, :]))
                QSL.wait(TE, qtok[h])
                if h > 0:
                    TE.wait_ge(sDV.h, free_q[h - 1])
                mm = None
                for k in range(KD):
                    mm = TE.matmul(PA[:, 0:512], qbuf[:, k * 128:k * 128 + 128],
                                   xn[:, k * 512:k * 512 + 512], start=(k == 0), stop=(k == KD - 1))
                qmm = sPE.inc(mm)
                GSL.wait(TE, gtok[h])
                if h > 0:
                    TE.wait_ge(sSE.h, free_g[h - 1])
                mm = None
                for k in range(KD):
                    mm = TE.matmul(PA[:, 512:1024], gbuf[:, k * 128:k * 128 + 128],
                                   xn[:, k * 512:k * 512 + 512], start=(k == 0), stop=(k == KD - 1))
                gmm = sPE.inc(mm)
                if h > 0:
                    TE.wait_ge(sDV.h, q2b_dv[h - 1])
                    mm = TE.matmul(PD[0:16, 512:1024], ind16[:, (h - 1) * 16:h * 16], E5[:, :],
                                   start=(h - 1 == 0), stop=False)
                    ssq_mm.append(sPE.inc(mm))
                if h + 1 < H:
                    SY.wait_ge(sPE.h, qmm)
                    qtok.append(QSL.dma(SY, out=qbuf[:, :], in_=wq_b[h + 1, :, :]))
                    SY.wait_ge(sPE.h, gmm)
                    gtok.append(GSL.dma(SY, out=gbuf[:, :], in_=wg_b[h + 1, :, :]))
                SEn.wait_ge(sPE.h, kmm)
                if h == 0:
                    SEn.wait_ge(sDV.h, dv_xnh[0])
                else:
                    SEn.wait_ge(sDV.h, u_done[h - 1])
                tk = SEn.activation(E1[:, :], PB[:, 0:512], AF.Tanh)
                free_k.append(sSE.n)
                SEn.wait_ge(sPE.h, amm)
                if h == 0:
                    SEn.wait_ge(sPO.h, sPO.n)
                asig = SEn.activation(a_f32[:, h * 512:h * 512 + 512], PC[:, 0:512],
                                      AF.Sigmoid, bias=b2col[:, 0:1])
                free_a.append(sSE.n)
                SEn.wait_ge(sPE.h, gmm)
                if h == 0:
                    OSL[0].wait(SEn, OSL[0].s.n)
                    OSL[1].wait(SEn, OSL[1].s.n)
                    SEn.wait_ge(sPO.h, sPO.n)
                else:
                    SEn.wait_ge(sDV.h, w1_done[h - 1])
                gse = SEn.activation(E2[:, :], PA[:, 512:1024], AF.Sigmoid)
                if h > 0:
                    SEn.wait_ge(sDV.h, w2_done[h - 1])
                g2se = SEn.activation(E4[:, :], PA[:, 512:1024], AF.Sigmoid, scale=-1.0)
                free_g.append(sSE.n)
                VE.wait_ge(sPE.h, qmm)
                if h > 0:
                    VE.wait_ge(sDV.h, q2b_dv[h - 1])
                    VE.wait_ge(sPE.h, ssq_mm[h - 1])
                qcp = VE.tensor_copy(E3[:, :], PA[:, 0:512])
                free_q.append(sDV.n)
                VE.wait_ge(sSE.h, tk)
                VE.wait_ge(sPE.h, vmm)
                u_done.append(VE.tensor_tensor(
                    u_bf[:, h * 512:h * 512 + 512], E1[:, :], PB[:, 512:1024], OP.mult))
                VE.wait_ge(sSE.h, g2se)
                w2_done.append(VE.tensor_tensor(
                    w2_bf[:, h * 512:h * 512 + 512], E4[:, :], PB[:, 512:1024], OP.mult))
                free_v.append(sDV.n)
                VE.wait_ge(sSE.h, gse)
                w1_done.append(VE.tensor_tensor(
                    w1_bf[:, h * 512:h * 512 + 512], E2[:, :], E3[:, :], OP.mult))
                q2b_dv.append(VE.tensor_tensor(E5[:, :], E3[:, :], E3[:, :], OP.mult))
                VE.wait_ge(sSE.h, asig)
                aclip_done.append(VE.tensor_scalar(
                    a_f32[:, h * 512:h * 512 + 512], a_f32[:, h * 512:h * 512 + 512],
                    0.9995, 0.6, OP.min, OP.max))
            TE.wait_ge(sDV.h, q2b_dv[H - 1])
            mm = TE.matmul(PD[0:16, 512:1024], ind16[:, (H - 1) * 16:H * 16], E5[:, :],
                           start=False, stop=True)
            ssq_last = sPE.inc(mm)

            VE.wait_ge(sPE.h, ssq_last)
            VE.tensor_copy(RQ16[:, :], PD[0:16, 512:1024])
            VE.tensor_scalar(RQ16[:, :], RQ16[:, :], 1.0 / 128, RMSEPS, OP.mult, OP.add)
            SEn.wait_ge(sDV.h, sDV.n)
            SEn.activation(S3[0:16, 0:512], RQ16[:, :], AF.Ln)
            SEn.activation(RQ16[:, :], S3[0:16, 0:512], AF.Exp, scale=-0.5)
            VE.wait_ge(sSE.h, sSE.n)
            rqb_dv = VE.tensor_copy(RQ16b[:, :], RQ16[:, :])
            bcast_mm, w1rq = [], []
            for h in range(H):
                if h == 0:
                    TE.wait_ge(sDV.h, rqb_dv)
                if h >= 2:
                    TE.wait_ge(sDV.h, w1rq[h - 2])
                mm = TE.matmul(PA[:, (h % 2) * 512:(h % 2) * 512 + 512],
                               indr[:, h * 128:h * 128 + 128], RQ16b[:, :], start=True, stop=True)
                bcast_mm.append(sPE.inc(mm))
                VE.wait_ge(sPE.h, bcast_mm[h])
                w1rq.append(VE.tensor_tensor(
                    w1_bf[:, h * 512:h * 512 + 512], w1_bf[:, h * 512:h * 512 + 512],
                    PA[:, (h % 2) * 512:(h % 2) * 512 + 512], OP.mult))

            out = {}
            if half == 0:
                out["dv_xnh"] = attn_apply(1, se_rows_ln2)
                SY.wait_ge(sPE.h, sPE.n)
                out["ktok"] = [KSL[0].dma(SY, out=kvaA[:, :], in_=wkva_b[0, :, :])]
            S2r = S2[:, :].rearrange("p (h t) -> p h t", h=H)
            S3r = S3[:, :].rearrange("p (h t) -> p h t", h=H)
            S4r = S4[:, :].rearrange("p (h t) -> p h t", h=H)
            a3 = a_f32.rearrange("p (h t) -> p h t", h=H)
            u3 = u_bf.rearrange("p (h t) -> p h t", h=H)
            w13 = w1_bf.rearrange("p (h t) -> p h t", h=H)
            w23 = w2_bf.rearrange("p (h t) -> p h t", h=H)
            y3 = y_bf.rearrange("p (h t) -> p h t", h=H)
            R3 = R_bf.rearrange("p (h t) -> p h t", h=H)
            scol_edge = sDV.n
            last_comb = None
            op_tail = None
            pool_y = [0] * NCHH
            pool_pf = [0] * NCHH
            SEn.wait_ge(sPE.h, sPE.n)
            SEn.wait_ge(CAT2.s.h, CAT2.s.n)
            wtk = [wosl[0].dma(SEn, out=wob[0][:, :], in_=wo_b[0, :, :]),
                   wosl[1].dma(SEn, out=wob[1][:, :], in_=wo_b[1, :, :])]
            RV = nc.vector
            for cc in range(NCHH):
                c = base_c + cc
                Sp = [S1, S5][cc % 2]
                Spr = Sp[:, :].rearrange("p (h t) -> p h t", h=H)
                if cc == 0:
                    VE.wait_ge(sDV.h, w1rq[-1])
                    VE.wait_ge(OSL[0].s.h, OSL[0].s.n)
                    VE.wait_ge(OSL[1].s.h, OSL[1].s.n)
                    VE.wait_ge(sPO.h, sPO.n)
                if cc >= 2 and (base_c + cc - 2) < NCORR:
                    RV.wait_ge(sPO.h, pool_pf[cc - 2])
                for hh in range(H):
                    o = hh * 512 + cc * 64
                    ins = RV.tensor_tensor_scan(Sp[:, hh * 64:hh * 64 + 64],
                                                a_f32[:, o:o + 64], a_f32[:, o:o + 64],
                                                1.0, OP.mult, OP.bypass)
                    if hh == H - 1:
                        sDV.inc(ins)
                VE.tensor_scalar(S3[:, :], Sp[:, :], 1e6, 1.0, OP.mult, OP.min)
                t_tok = VE.tensor_tensor(S3r[:, :, :], u3[:, :, cc * 64:cc * 64 + 64],
                                         S3r[:, :, :], OP.mult)
                Std = [S4, S2][cc % 2]
                Stdr = [S4r, S2r][cc % 2]
                RV.wait_ge(sDV.h, t_tok)
                RV.wait_ge(sDV.h, scol_edge)
                if cc > 1:
                    RV.wait_ge(sPO.h, pool_y[cc - 2])
                for hh in range(H):
                    o = hh * 512 + cc * 64
                    ins = RV.tensor_tensor_scan(Std[:, hh * 64:hh * 64 + 64],
                                                a_f32[:, o:o + 64], S3[:, hh * 64:hh * 64 + 64],
                                                scol[:, hh:hh + 1], OP.mult, OP.add)
                    if hh == H - 1:
                        sDV.inc(ins)
                Acur = [AcA, AcB2][c % 2]
                Anxt = [AcA, AcB2][(c + 1) % 2]
                scol_edge = VE.tensor_copy(scol[:, :].unsqueeze(2), Stdr[:, :, 63:64])
                if cc >= 1 and (c - 1) < NCORR:
                    VE.wait_ge(sPO.h, pool_pf[cc - 1])
                VE.tensor_tensor(Anxt[:, :].unsqueeze(2), Acur[:, :].unsqueeze(2),
                                 Spr[:, :, 63:64], OP.mult)
                PL.wait_ge(sDV.h, scol_edge)
                if c < NCORR:
                    AcBc = Acur[:, :].unsqueeze(2).broadcast_to([128, H, 64])
                    a3c = a3[:, :, cc * 64:cc * 64 + 64]
                    PL.tensor_tensor(a3c, Spr[:, :, :], AcBc, OP.mult)
                    pool_pf[cc] = PL.tensor_tensor(R3[:, :, c * 64:c * 64 + 64],
                                                   w13[:, :, cc * 64:cc * 64 + 64], a3c, OP.mult)
                PL.tensor_tensor(y3[:, :, cc * 64:cc * 64 + 64],
                                 w13[:, :, cc * 64:cc * 64 + 64], Stdr[:, :, :], OP.mult)
                last_comb = pool_y[cc] = PL.tensor_tensor(
                    y3[:, :, cc * 64:cc * 64 + 64], y3[:, :, cc * 64:cc * 64 + 64],
                    w23[:, :, cc * 64:cc * 64 + 64], OP.add)
                if cc == NCHH - 1 and publish:
                    PO.wait_ge(sDV.h, scol_edge + 1)
                    ccA = cc_in.ap()[0:1, :].rearrange("r (d h) -> (r d) h", d=128)
                    ccB = cc_in.ap()[1:2, :].rearrange("r (d h) -> (r d) h", d=128)
                    CP.dma(PO, out=ccA, in_=AcA[:, :])
                    pubtok = CP.dma(PO, out=ccB, in_=scol[:, :])
                    PO.wait_ge(CP.s.h, pubtok)
                    PO.collective_compute(
                        "AllGather", OP.bypass, replica_groups=[list(range(NCORES))],
                        ins=[cc_in.ap().opt()], outs=[cc_out.ap().opt()],
                    ).then_inc(sCC.h)
                    sCC.n += 1
                if cc == 3 or cc == NCHH - 1:
                    op_tail = outproj_group(half, cc // 4, last_comb, wtk, W=256)
                    if cc < NCHH - 1:
                        SEn.wait_ge(sPE.h, sPE.n)
                        wtk = [wosl[0].dma(SEn, out=wob[0][:, :], in_=wo_b[0, :, :]),
                               wosl[1].dma(SEn, out=wob[1][:, :], in_=wo_b[1, :, :])]
            if half == 0:
                SY.wait_ge(sPE.h, sPE.n)
                out["qtok"] = [QSL.dma(SY, out=qbuf[:, :], in_=wq_b[0, :, :])]
                out["gtok"] = [GSL.dma(SY, out=gbuf[:, :], in_=wg_b[0, :, :])]
            out["op_tail"] = op_tail
            return out

        if STAGE >= 1:
            wtok1 = [WSL[0].dma(PO, out=WB[:, 0:4096], in_=wgu[0, :, :]),
                     WSL[1].dma(PO, out=WB[:, 4096:8192], in_=wgu[1, :, :])]
        pe_ln1 = ln_stats_full(x_in)
        se_rows = ln_rows(pe_ln1)
        dv_bc = ln_bcast(se_rows)
        dv_xn = ln_apply_full(x_in, dv_bc)
        if STAGE == 0:
            PO.wait_ge(sDV.h, dv_xn)
            t1 = None
            for g in range(KD):
                t1 = CP.dma(PO, out=out_d[g * 128:(g + 1) * 128, :],
                            in_=xn[:, g * 1024:(g + 1) * 1024])
            PO.wait_ge(CP.s.h, t1)
            SEn.wait_ge(CP.s.h, t1)
        if STAGE >= 1:
            ffn1_dst = out_d if STAGE == 1 else x1sp
            cast_jobs = []
            if STAGE >= 2:
                for h in range(H):
                    cast_jobs.append((wkva_b[h, :, 0:3072], wkva[h, :, 0:3072]))
                    cast_jobs.append((wkva_b[h, :, 3072:6144], wkva[h, :, 3072:6144]))
                for h in range(H):
                    cast_jobs.append((wq_b[h, :, :], wq_d[h, :, :]))
                    cast_jobs.append((wg_b[h, :, :], wg_d[h, :, :]))
            pe_ln2 = ffn_phase(x_in, ffn1_dst, dv_xn, fuse_ln=(STAGE >= 2), wtok=wtok1,
                               casts=cast_jobs)
        if STAGE >= 2:
            se_rows2 = ln_rows(pe_ln2)
            r0 = attn_half(0, se_rows2)
            attn_half(1, se_rows2, publish=True, carry=r0)
            PO.wait_ge(sCC.h, sCC.n)
            gv = cc_out.ap().rearrange("r (d h) -> r d h", d=128).transpose([1, 0, 2])
            gtok = CP.dma(PO, out=gab[:, :].rearrange("p (r h) -> p r h", r=2 * NCORES), in_=gv)
            CP.wait(VE, gtok)
            for j in range(7):
                VE.tensor_scalar(T1[:, :], gab[:, 2 * j * 16:(2 * j + 1) * 16],
                                 mskT[:, j:j + 1], 0.0, OP.mult, OP.add)
                VE.tensor_scalar(T1[:, :], T1[:, :], mskT[:, 8 + j:9 + j], 0.0, OP.add, OP.add)
                VE.tensor_tensor(sin_t[:, :], sin_t[:, :], T1[:, :], OP.mult)
                VE.tensor_scalar(T2[:, :], gab[:, (2 * j + 1) * 16:(2 * j + 2) * 16],
                                 mskT[:, j:j + 1], 0.0, OP.mult, OP.add)
                VE.tensor_tensor(sin_t[:, :], sin_t[:, :], T2[:, :], OP.add)
            VE.wait_ge(sPO.h, sPO.n)
            sinB = sin_t[:, :].unsqueeze(2).broadcast_to([128, H, NCORR * 64])
            R3f = R_bf.rearrange("p (h t) -> p h t", h=H)
            c3 = corr_bf.rearrange("p (h t) -> p h t", h=H)
            corr_dv = (VE.tensor_tensor(c3[:, :, :], R3f[:, :, :], sinB, OP.mult))
            SEn.wait_ge(sPE.h, sPE.n)
            wob2 = [WB[:, 0:2048], WB[:, 2048:4096]]
            wtk2 = [wosl[0].dma(SEn, out=wob2[0][:, :], in_=wo_b[0, :, :]),
                    wosl[1].dma(SEn, out=wob2[1][:, :], in_=wo_b[1, :, :])]
            SY.wait_ge(sDV.h, sDV.n)
            xtk2 = [STS[0].dma(SY, out=st_bufs[0][:, :], in_=x2sp[0:128, :]),
                    STS[1].dma(SY, out=st_bufs[1][:, :], in_=x2sp[128:256, :])]
            op2_mm, op2_evac, op2_st, stat_mm, xn_cast = [], [], [], [], []
            for m2 in range(KD):
                pd_ = PB[:, 0:256] if (m2 % 2 == 0) else PB[:, 512:768]
                wosl[m2 % 2].wait(TE, wtk2[m2])
                if m2 == 0:
                    TE.wait_ge(sDV.h, corr_dv)
                if m2 >= 2:
                    TE.wait_ge(sDV.h, op2_evac[m2 - 2])
                mm = None
                for k in range(KD):
                    mm = TE.matmul(pd_, wob2[m2 % 2][:, k * 128:k * 128 + 128],
                                   corr_bf[:, k * 256:k * 256 + 256], start=(k == 0), stop=(k == KD - 1))
                op2_mm.append(sPE.inc(mm))
                if m2 + 2 < KD:
                    SEn.wait_ge(sPE.h, op2_mm[m2])
                    wtk2.append(wosl[m2 % 2].dma(SEn, out=wob2[m2 % 2][:, :], in_=wo_b[m2 + 2, :, :]))
                ob = [S4, S5][m2 % 2]
                VE.wait_ge(sPE.h, op2_mm[m2])
                STS[m2 % 2].wait(VE, xtk2[m2])
                OSL[m2 % 2].wait(VE, OSL[m2 % 2].s.n)
                op2_evac.append((VE.tensor_tensor(
                    ob[:, 0:256], pd_, st_bufs[m2 % 2][:, 0:256], OP.add)))
                if m2 > 0:
                    SEn.wait_ge(sPE.h, stat_mm[m2 - 1])
                    SEn.wait_ge(sPO.h, xn_cast[m2 - 1])
                SEn.wait_ge(sDV.h, op2_evac[m2])
                SEn.copy(l1v[:, 0:256], ob[:, 0:256])
                SEn.copy(l1v[:, 256:1024], st_bufs[m2 % 2][:, 256:1024])
                SEn.activation(l2v[:, 0:256], ob[:, 0:256], AF.Square)
                sqt = SEn.activation(l2v[:, 256:1024], st_bufs[m2 % 2][:, 256:1024], AF.Square)
                TE.wait_ge(sSE.h, sqt)
                mm = None
                for c0 in (0, 512):
                    TE.matmul(PC[0:1, c0:c0 + 512], ones_col[:, :], l1v[:, c0:c0 + 512],
                              start=(m2 == 0), stop=(m2 == KD - 1))
                    mm = TE.matmul(PD[0:1, c0:c0 + 512], ones_col[:, :], l2v[:, c0:c0 + 512],
                                   start=(m2 == 0), stop=(m2 == KD - 1))
                stat_mm.append(sPE.inc(mm))
                if m2 + 2 < KD:
                    SY.wait_ge(sDV.h, op2_evac[m2])
                    SY.wait_ge(sSE.h, sqt)
                    xtk2.append(STS[m2 % 2].dma(
                        SY, out=st_bufs[m2 % 2][:, :],
                        in_=x2sp[(m2 + 2) * 128:(m2 + 3) * 128, :]))
                PL.wait_ge(sSE.h, sqt)
                PL.wait_ge(sPE.h, sPE.n)
                xn_cast.append(PL.tensor_copy(xn[:, m2 * 1024:(m2 + 1) * 1024], l1v[:, :]))
                SEn.wait_ge(sDV.h, op2_evac[m2])
                op2_st.append(OSL[m2 % 2].dma(
                    SEn, out=x2sp[m2 * 128:(m2 + 1) * 128, 0:256], in_=ob[:, 0:256]))
        if STAGE == 2:
            SY.wait_ge(OSL[0].s.h, OSL[0].s.n)
            SY.wait_ge(OSL[1].s.h, OSL[1].s.n)
            SY.wait_ge(sDV.h, sDV.n)
            btoks = []
            for g in range(KD):
                if g >= 2:
                    OSL[g % 2].wait(SY, btoks[g - 2][1])
                lt = STS[g % 2].dma(SY, out=st_bufs[g % 2][:, :],
                                    in_=x2sp[g * 128:(g + 1) * 128, :])
                STS[g % 2].wait(SEn, lt)
                stt = OSL[g % 2].dma(SEn, out=out_d[g * 128:(g + 1) * 128, :],
                                     in_=st_bufs[g % 2][:, :])
                btoks.append((lt, stt))
        if STAGE >= 3:
            PO.wait_ge(sPE.h, sPE.n)
            wtok2 = [WSL[0].dma(PO, out=WB[:, 0:4096], in_=wgu[0, :, :]),
                     WSL[1].dma(PO, out=WB[:, 4096:8192], in_=wgu[1, :, :])]
            se_rows3 = ln_rows(stat_mm[-1])
            dv_bc3 = ln_bcast(se_rows3)
            dv_xn3 = ln_apply_full(x2sp, dv_bc3)
            ffn_phase(x2sp, out_d, dv_xn3, fuse_ln=False, wtok=wtok2)
        SEn.wait_ge(OSL[0].s.h, OSL[0].s.n)
        SEn.wait_ge(OSL[1].s.h, OSL[1].s.n)
    es.close()
    return nc


def _tiles(w):
    O, I = w.shape
    nm, nk = O // 128, I // 128
    t = w.reshape(nm, 128, nk, 128).transpose(0, 3, 2, 1)
    return np.ascontiguousarray(t).reshape(nm, 128, I)


def kernel(L0_attn_x, L0_attn_w_in, L0_attn_w_out, L0_ffn_gate_w, L0_ffn_up_w,
           L0_ffn_down_w, L0_ffn_gate_s, L0_ffn_up_s, L0_ffn_down_s):
    x = np.ascontiguousarray(np.asarray(L0_attn_x, dtype=np.float32))
    w_in = np.asarray(L0_attn_w_in, dtype=np.float32)
    w_out = np.asarray(L0_attn_w_out, dtype=np.float32)
    gate_w = np.asarray(L0_ffn_gate_w, dtype=np.float32)
    up_w = np.asarray(L0_ffn_up_w, dtype=np.float32)
    down_w = np.asarray(L0_ffn_down_w, dtype=np.float32)
    gate_s = np.asarray(L0_ffn_gate_s, dtype=np.float32)
    up_s = np.asarray(L0_ffn_up_s, dtype=np.float32)
    down_s = np.asarray(L0_ffn_down_s, dtype=np.float32)

    tg = _tiles(gate_w)
    tu = _tiles(up_w)
    wgu_h = np.ascontiguousarray(np.concatenate([tg, tu], axis=2))
    wd_h = _tiles(down_w)
    wit = _tiles(w_in)
    wkva_h = np.ascontiguousarray(np.concatenate([wit[16:32], wit[32:48], wit[48:64]], axis=2))
    wq_h = np.ascontiguousarray(wit[0:16])
    wg_h = np.ascontiguousarray(wit[64:80])
    wo_h = _tiles(w_out)
    gs2 = np.ascontiguousarray(gate_s.reshape(KI, 128).T)
    us2 = np.ascontiguousarray(up_s.reshape(KI, 128).T)
    ds2 = np.ascontiguousarray(down_s.reshape(KD, 128).T)
    ind16_h = np.zeros((128, 16 * H), dtype=np.float32)
    for h in range(H):
        ind16_h[:, h * 16 + h] = 1.0
    indr_h = np.zeros((16, 128 * H), dtype=np.float32)
    for h in range(H):
        indr_h[h, h * 128:(h + 1) * 128] = 1.0

    nc = build_program()
    in_maps = []
    for c in range(NCORES):
        b, q = c // 4, c % 4
        xs = np.ascontiguousarray(x[b, q * TPC:(q + 1) * TPC, :].T)
        msk = np.zeros((128, 16), dtype=np.float32)
        for j in range(8):
            beta = 1.0 if (j // 4 == b and j < c) else 0.0
            msk[:, j] = beta
            msk[:, 8 + j] = 1.0 - beta
        in_maps.append({
            "x": xs, "wgu": wgu_h, "wd": wd_h, "wkva": wkva_h, "wq": wq_h, "wgp": wg_h,
            "wo": wo_h, "gs": gs2, "us": us2, "ds": ds2, "msk": msk,
            "ind16": ind16_h, "indr": indr_h,
        })
    res = run_bass_kernel_spmd(nc, in_maps, core_ids=list(range(NCORES)))
    out = np.empty((B, S, D), dtype=np.float32)
    for c in range(NCORES):
        b, q = c // 4, c % 4
        out[b, q * TPC:(q + 1) * TPC, :] = res.results[c]["out"].T
    return out


# version 18 (final)
# speedup vs baseline: 1.0417x; 1.0335x over previous
import numpy as np
import concourse.bass as bass
import concourse.mybir as mybir
from concourse.bass_utils import run_bass_kernel_spmd
from contextlib import ExitStack

F32 = mybir.dt.float32
BF16 = mybir.dt.bfloat16
AF = mybir.ActivationFunctionType
OP = mybir.AluOpType

NCORES = 8
B, S, D = 2, 4096, 2048
TPC = 1024
HALF = 512
KD = D // 128
INTER = 5632
KI = INTER // 128
H = 16
CH = 64
NCH = TPC // CH
NCHH = HALF // CH
NCORR = 4
LNEPS = 1e-5
RMSEPS = 1.1920928955078125e-07
CLAMP = 1e-6

STAGE = 3


class Sem:
    def __init__(self, h):
        self.h = h
        self.n = 0

    def inc(self, instr, k=1):
        instr.then_inc(self.h, k)
        self.n += k
        return self.n


class DSlot:
    def __init__(self, sem):
        self.s = sem

    def dma(self, eng, **kw):
        return self.s.inc(eng.dma_start(**kw), 16)

    def wait(self, eng, tok):
        eng.wait_ge(self.s.h, tok)


def build_program():
    nc = bass.Bass("TRN2", target_bir_lowering=False, debug=False)
    x_in = nc.dram_tensor("x", [D, TPC], F32, kind="ExternalInput").ap()
    wgu = nc.dram_tensor("wgu", [KI, 128, 2 * D], F32, kind="ExternalInput").ap()
    wd = nc.dram_tensor("wd", [KD, 128, INTER], F32, kind="ExternalInput").ap()
    wkva = nc.dram_tensor("wkva", [H, 128, 3 * D], F32, kind="ExternalInput").ap()
    wq_d = nc.dram_tensor("wq", [H, 128, D], F32, kind="ExternalInput").ap()
    wg_d = nc.dram_tensor("wgp", [H, 128, D], F32, kind="ExternalInput").ap()
    wo_d = nc.dram_tensor("wo", [KD, 128, D], F32, kind="ExternalInput").ap()
    gs_d = nc.dram_tensor("gs", [128, KI], F32, kind="ExternalInput").ap()
    us_d = nc.dram_tensor("us", [128, KI], F32, kind="ExternalInput").ap()
    ds_d = nc.dram_tensor("ds", [128, KD], F32, kind="ExternalInput").ap()
    msk_d = nc.dram_tensor("msk", [128, 16], F32, kind="ExternalInput").ap()
    ind16_d = nc.dram_tensor("ind16", [128, 16 * H], F32, kind="ExternalInput").ap()
    indr_d = nc.dram_tensor("indr", [16, 128 * H], F32, kind="ExternalInput").ap()
    out_d = nc.dram_tensor("out", [D, TPC], F32, kind="ExternalOutput").ap()
    x1sp = nc.dram_tensor("x1sp", [D, TPC], F32).ap()
    x2sp = nc.dram_tensor("x2sp", [D, TPC], F32).ap()
    wkva_b = nc.dram_tensor("wkva_b", [H, 128, 3 * D], BF16).ap()
    wq_b = nc.dram_tensor("wq_b", [H, 128, D], BF16).ap()
    wg_b = nc.dram_tensor("wg_b", [H, 128, D], BF16).ap()
    wo_b = nc.dram_tensor("wo_b", [KD, 128, D], BF16).ap()
    cc_in = nc.dram_tensor("cc_in", [2, D], F32)
    cc_out = nc.dram_tensor("cc_out", [2 * NCORES, D], F32)

    es = ExitStack()
    sbt = lambda name, shape, dt: es.enter_context(nc.sbuf_tensor(name, shape, dt))
    pst = lambda name, shape: es.enter_context(nc.psum_tensor(name, shape, F32))
    mksem = lambda name: Sem(es.enter_context(nc.semaphore(name)))
    mkslot = lambda name: DSlot(mksem(name))

    xn = sbt("xn", [128, 16384], BF16)
    AR = sbt("AR", [128, 53248], BF16)
    S1 = sbt("S1", [128, 1024], F32)
    S2 = sbt("S2", [128, 1024], F32)
    S3 = sbt("S3", [128, 1024], F32)
    S4 = sbt("S4", [128, 1024], F32)
    S5 = sbt("S5", [128, 1024], F32)
    E1 = sbt("E1", [128, 512], F32)
    E2 = sbt("E2", [128, 512], F32)
    E3 = sbt("E3", [128, 512], F32)
    E4 = sbt("E4", [128, 512], F32)
    E5 = sbt("E5", [128, 512], BF16)
    ST1 = sbt("ST1", [128, 1024], F32)
    ST2 = sbt("ST2", [128, 1024], F32)
    WB = sbt("WB", [128, 8192], BF16)
    r_mean = sbt("r_mean", [1, 1024], F32)
    r_rstd = sbt("r_rstd", [1, 1024], F32)
    RQ16 = sbt("RQ16", [16, 512], F32)
    RQ16b = sbt("RQ16b", [16, 512], BF16)
    ones_col = sbt("ones_col", [128, 1], BF16)
    ones_row = sbt("ones_row", [1, 128], F32)
    b2col = sbt("b2col", [128, 1], F32)
    epscol = sbt("epscol", [128, 1], F32)
    gsT = sbt("gsT", [128, KI], F32)
    usT = sbt("usT", [128, KI], F32)
    dsT = sbt("dsT", [128, KD], F32)
    mskT = sbt("mskT", [128, 16], F32)
    ind16 = sbt("ind16s", [128, 16 * H], BF16)
    indr = sbt("indrs", [16, 128 * H], BF16)
    AcA = sbt("AcA", [128, 16], F32)
    AcB2 = sbt("AcB2", [128, 16], F32)
    scol = sbt("scol", [128, 16], F32)
    sin_t = sbt("sin", [128, 16], F32)
    gab = sbt("gab", [128, 256], F32)
    T1 = sbt("T1", [128, 16], F32)
    T2 = sbt("T2", [128, 16], F32)

    PA = pst("PA", [128, 1024])
    PB = pst("PB", [128, 1024])
    PC = pst("PC", [128, 1024])
    PD = pst("PD", [128, 1024])

    sPE = mksem("sPE")
    sDV = mksem("sDV")
    sSE = mksem("sSE")
    sPO = mksem("sPO")
    sCC = mksem("sCC")
    STS = [mkslot("sST0"), mkslot("sST1")]
    WSL = [mkslot("sWA"), mkslot("sWB")]
    DSL = [mkslot("sDA"), mkslot("sDB")]
    KSL = [mkslot("sKA"), mkslot("sKB")]
    QSL = mkslot("sQ")
    GSL = mkslot("sG")
    OSL = [mkslot("sOA"), mkslot("sOB")]
    CW = mkslot("sCW")
    CP = mkslot("sCP")
    CAT = mkslot("sCT")
    CAT2 = mkslot("sCT2")
    WOS = [mkslot("sWO0"), mkslot("sWO1")]

    TE, PO, SY = nc.tensor, nc.gpsimd, nc.sync

    class OEng:
        def __init__(self, eng, sem):
            self._e = eng
            self._s = sem

        def wait_ge(self, h, v):
            self._e.wait_ge(h, v)

        def dma_start(self, **kw):
            return self._e.dma_start(**kw)

        def __getattr__(self, name):
            f = getattr(self._e, name)

            def wrapped(*a, **k):
                self._e.wait_ge(self._s.h, self._s.n)
                return self._s.inc(f(*a, **k))
            return wrapped

    VE = OEng(nc.vector, sDV)
    SEn = OEng(nc.scalar, sSE)
    PL = OEng(nc.gpsimd, sPO)

    a_f32 = AR[:, 0:16384].bitcast(F32)
    u_bf = AR[:, 16384:24576]
    w1_bf = AR[:, 24576:32768]
    w2_bf = AR[:, 32768:40960]
    y_bf = AR[:, 40960:49152]
    R_bf = AR[:, 49152:53248]
    hdn = AR[:, 0:KI * 1024]
    corr_bf = u_bf[:, 0:4096]
    l1v = AR[:, 45056:46080]
    l2v = AR[:, 46080:47104]
    kvaA = WB[:, 0:6144]
    kvaB = xn[:, 8192:14336]
    qbuf = WB[:, 6144:8192]
    gbuf = xn[:, 14336:16384]
    dwb = [xn[:, 0:5632], xn[:, 5632:11264]]
    wob = [qbuf, gbuf]
    wosl = WOS
    st_bufs = [ST1, ST2]

    with nc.Block() as blk:
        VE.memset(ones_col[:, :], 1.0)
        VE.memset(ones_row[:, :], 1.0)
        VE.memset(b2col[:, :], 2.0)
        VE.memset(epscol[:, :], LNEPS)
        VE.memset(scol[:, :], 0.0)
        VE.memset(AcA[:, :], 1.0)
        VE.memset(AcB2[:, :], 1.0)
        (VE.memset(sin_t[:, :], 0.0))
        CW.dma(SY, out=gsT[:, :], in_=gs_d)
        CW.dma(SY, out=usT[:, :], in_=us_d)
        CW.dma(SY, out=dsT[:, :], in_=ds_d)
        consts_w = CW.dma(SY, out=mskT[:, :], in_=msk_d)
        CP.dma(PO, out=ind16[:, :], in_=ind16_d)
        consts_p = CP.dma(PO, out=indr[:, :], in_=indr_d)

        def ln_stats_full(src_dram):
            SY.wait_ge(sDV.h, sDV.n)
            VE.wait_ge(sPE.h, sPE.n)
            toks = [STS[0].dma(SY, out=st_bufs[0][:, :], in_=src_dram[0:128, :]),
                    STS[1].dma(SY, out=st_bufs[1][:, :], in_=src_dram[128:256, :])]
            casts = []
            smms = []
            for g in range(KD):
                STS[g % 2].wait(VE, toks[g])
                if g > 0:
                    VE.wait_ge(sPE.h, smms[g - 1])
                cst = VE.tensor_copy(xn[:, g * 1024:(g + 1) * 1024], st_bufs[g % 2][:, :])
                if g > 0:
                    SEn.wait_ge(sPE.h, smms[g - 1])
                STS[g % 2].wait(SEn, toks[g])
                casts.append(SEn.activation(l2v[:, :], st_bufs[g % 2][:, :], AF.Square))
                TE.wait_ge(sDV.h, cst)
                TE.wait_ge(sSE.h, casts[g])
                mm = None
                for c0 in (0, 512):
                    TE.matmul(PC[0:1, c0:c0 + 512], ones_col[:, :],
                              xn[:, g * 1024 + c0:g * 1024 + c0 + 512],
                              start=(g == 0), stop=(g == KD - 1))
                    mm = TE.matmul(PD[0:1, c0:c0 + 512], ones_col[:, :], l2v[:, c0:c0 + 512],
                                   start=(g == 0), stop=(g == KD - 1))
                smms.append(sPE.inc(mm))
                if g + 2 < KD:
                    SY.wait_ge(sDV.h, cst)
                    SY.wait_ge(sSE.h, casts[g])
                    toks.append(STS[g % 2].dma(
                        SY, out=st_bufs[g % 2][:, :],
                        in_=src_dram[(g + 2) * 128:(g + 3) * 128, :]))
            return smms[-1]

        def ln_rows(pe_done):
            VE.wait_ge(sPE.h, pe_done)
            VE.wait_ge(OSL[0].s.h, OSL[0].s.n)
            VE.wait_ge(OSL[1].s.h, OSL[1].s.n)
            VE.tensor_scalar(r_mean[:, :], PC[0:1, :], 1.0 / D, 0.0, OP.mult, OP.add)
            VE.tensor_scalar(S3[0:1, :], PD[0:1, :], 1.0 / D, 0.0, OP.mult, OP.add)
            VE.tensor_tensor(S4[0:1, :], r_mean[:, :], r_mean[:, :], OP.mult)
            (VE.tensor_tensor(S3[0:1, :], S3[0:1, :], S4[0:1, :], OP.subtract))
            SEn.wait_ge(sDV.h, sDV.n)
            SEn.activation(S4[0:1, :], S3[0:1, :], AF.Ln, bias=epscol[0:1, 0:1])
            return (SEn.activation(r_rstd[:, :], S4[0:1, :], AF.Exp, scale=-0.5))

        def ln_bcast(se_rows, c0=0, n=1024, guard_dv=None):
            TE.wait_ge(sSE.h, se_rows)
            if guard_dv is not None:
                TE.wait_ge(sDV.h, guard_dv)
            mm = None
            for cc in range(0, n, 512):
                TE.matmul(PC[:, cc:cc + 512], ones_row[:, :], r_mean[0:1, c0 + cc:c0 + cc + 512],
                          start=True, stop=True)
                mm = TE.matmul(PD[:, cc:cc + 512], ones_row[:, :], r_rstd[0:1, c0 + cc:c0 + cc + 512],
                               start=True, stop=True)
            pe = sPE.inc(mm)
            VE.wait_ge(sPE.h, pe)
            VE.tensor_copy(S1[:, 0:n], PC[:, 0:n])
            return (VE.tensor_copy(S2[:, 0:n], PD[:, 0:n]))

        def ln_apply_full(src_dram, dv_bc):
            dv_tok = po_tok = None
            for g in range(KD):
                if g == 0:
                    VE.wait_ge(sDV.h, dv_bc)
                    PL.wait_ge(sDV.h, dv_bc)
                    PL.wait_ge(sPE.h, sPE.n)
                if g % 2 == 0:
                    VE.tensor_tensor(S3[:, :], xn[:, g * 1024:(g + 1) * 1024], S1[:, :], OP.subtract)
                    dv_tok = VE.tensor_tensor(
                        xn[:, g * 1024:(g + 1) * 1024], S3[:, :], S2[:, :], OP.mult)
                else:
                    PL.tensor_tensor(S4[:, :], xn[:, g * 1024:(g + 1) * 1024], S1[:, :], OP.subtract)
                    po_tok = PL.tensor_tensor(
                        xn[:, g * 1024:(g + 1) * 1024], S4[:, :], S2[:, :], OP.mult)
            return (dv_tok, po_tok)

        def ffn_phase(xsrc_dram, xdst_dram, dv_xn, fuse_ln, wtok=None, casts=None):
            casts = list(casts or [])
            SEn.wait_ge(CW.s.h, consts_w)
            VE.wait_ge(CW.s.h, consts_w)
            if wtok is None:
                PO.wait_ge(sPE.h, sPE.n)
                wtok = [WSL[0].dma(PO, out=WB[:, 0:4096], in_=wgu[0, :, :]),
                        WSL[1].dma(PO, out=WB[:, 4096:8192], in_=wgu[1, :, :])]
            gu_mm, gu_evac = [], []
            for m in range(KI):
                pg = PA if (m % 2 == 0) else PC
                pu = PB if (m % 2 == 0) else PD
                wbuf = WB[:, (m % 2) * 4096:(m % 2) * 4096 + 4096]
                WSL[m % 2].wait(TE, wtok[m])
                if m == 0:
                    TE.wait_ge(sDV.h, dv_xn[0])
                    TE.wait_ge(sPO.h, dv_xn[1])
                if m >= 2:
                    TE.wait_ge(sDV.h, gu_evac[m - 2])
                mm = None
                for k in range(KD):
                    for cc in (0, 512):
                        TE.matmul(pg[:, cc:cc + 512], wbuf[:, k * 128:k * 128 + 128],
                                  xn[:, k * 1024 + cc:k * 1024 + cc + 512],
                                  start=(k == 0), stop=(k == KD - 1))
                for k in range(KD):
                    for cc in (0, 512):
                        mm = TE.matmul(pu[:, cc:cc + 512], wbuf[:, 2048 + k * 128:2048 + k * 128 + 128],
                                       xn[:, k * 1024 + cc:k * 1024 + cc + 512],
                                       start=(k == 0), stop=(k == KD - 1))
                gu_mm.append(sPE.inc(mm))
                if m + 2 < KI:
                    PO.wait_ge(sPE.h, gu_mm[m])
                    wtok.append(WSL[m % 2].dma(
                        PO, out=WB[:, (m % 2) * 4096:(m % 2) * 4096 + 4096],
                        in_=wgu[m + 2, :, :]))
                if casts:
                    dst, src = casts.pop(0)
                    CAT.dma(PO, out=dst, in_=src)
                sg = [S4, S5][m % 2]
                SEn.wait_ge(sPE.h, gu_mm[m])
                if m >= 2:
                    SEn.wait_ge(sDV.h, gu_evac[m - 2])
                sse_m = (SEn.activation(sg[:, :], pg[:, :], AF.Silu, scale=gsT[:, m:m + 1]))
                VE.wait_ge(sPE.h, gu_mm[m])
                VE.tensor_scalar(S3[:, :], pu[:, :], usT[:, m:m + 1], 0.0, OP.mult, OP.add)
                VE.wait_ge(sSE.h, sse_m)
                gu_evac.append((VE.tensor_tensor(
                    hdn[:, m * 1024:(m + 1) * 1024], sg[:, :], S3[:, :], OP.mult)))
            hdn_ready = gu_evac[-1]
            PO.wait_ge(sPE.h, gu_mm[KI - 1])
            dtok = [DSL[0].dma(PO, out=dwb[0][:, :], in_=wd[0, :, :]),
                    DSL[1].dma(PO, out=dwb[1][:, :], in_=wd[1, :, :])]
            SY.wait_ge(sDV.h, sDV.n)
            xtok = [STS[0].dma(SY, out=st_bufs[0][:, :], in_=xsrc_dram[0:128, :]),
                    STS[1].dma(SY, out=st_bufs[1][:, :], in_=xsrc_dram[128:256, :])]
            dn_mm, dn_evac, dn_st, fmms, csts = [], [], [], [], []
            lbs = [(l1v, l2v), (AR[:, 47104:48128], AR[:, 48128:49152])]
            for m2 in range(KD):
                pd_ = PA if (m2 % 2 == 0) else PB
                DSL[m2 % 2].wait(TE, dtok[m2])
                if m2 == 0:
                    TE.wait_ge(sDV.h, hdn_ready)
                if m2 >= 2:
                    TE.wait_ge(sDV.h, dn_evac[m2 - 2])
                mm = None
                for k in range(KI):
                    for cc in (0, 512):
                        mm = TE.matmul(pd_[:, cc:cc + 512], dwb[m2 % 2][:, k * 128:k * 128 + 128],
                                       hdn[:, k * 1024 + cc:k * 1024 + cc + 512],
                                       start=(k == 0), stop=(k == KI - 1))
                dn_mm.append(sPE.inc(mm))
                if m2 + 2 < KD:
                    PO.wait_ge(sPE.h, dn_mm[m2])
                    dtok.append(DSL[m2 % 2].dma(PO, out=dwb[m2 % 2][:, :], in_=wd[m2 + 2, :, :]))
                for _ in range(2):
                    if casts:
                        dst, src = casts.pop(0)
                        CAT.dma(PO, out=dst, in_=src)
                ob = [S4, S5][m2 % 2]
                VE.wait_ge(sPE.h, dn_mm[m2])
                STS[m2 % 2].wait(VE, xtok[m2])
                OSL[m2 % 2].wait(VE, OSL[m2 % 2].s.n)
                VE.tensor_scalar(S3[:, :], pd_[:, :], dsT[:, m2:m2 + 1], 0.5, OP.mult, OP.mult)
                dn_evac.append((VE.tensor_tensor(
                    ob[:, :], S3[:, :], st_bufs[m2 % 2][:, :], OP.add)))
                if m2 + 2 < KD:
                    SY.wait_ge(sDV.h, dn_evac[m2])
                    xtok.append(STS[m2 % 2].dma(
                        SY, out=st_bufs[m2 % 2][:, :],
                        in_=xsrc_dram[(m2 + 2) * 128:(m2 + 3) * 128, :]))
                SEn.wait_ge(sDV.h, dn_evac[m2])
                dn_st.append(OSL[m2 % 2].dma(
                    SEn, out=xdst_dram[m2 * 128:(m2 + 1) * 128, :], in_=ob[:, :]))
                if fuse_ln:
                    if m2 > 1:
                        VE.wait_ge(sPE.h, fmms[-1])
                    VE.tensor_copy(lbs[m2 % 2][0][:, :], ob[:, :])
                    csts.append(VE.tensor_tensor(
                        lbs[m2 % 2][1][:, :], ob[:, :], ob[:, :], OP.mult))

                    def emit_stats(j):
                        TE.wait_ge(sDV.h, csts[j])
                        mm = None
                        for c0 in (0, 512):
                            TE.matmul(PC[0:1, c0:c0 + 512], ones_col[:, :],
                                      lbs[j % 2][0][:, c0:c0 + 512],
                                      start=(j == 0), stop=(j == KD - 1))
                            mm = TE.matmul(PD[0:1, c0:c0 + 512], ones_col[:, :],
                                           lbs[j % 2][1][:, c0:c0 + 512],
                                           start=(j == 0), stop=(j == KD - 1))
                        fmms.append(sPE.inc(mm))
                    if m2 > 0:
                        emit_stats(m2 - 1)
                    if m2 == KD - 1:
                        emit_stats(m2)
            while casts:
                dst, src = casts.pop(0)
                CAT.dma(PO, out=dst, in_=src)
            return fmms[-1] if fuse_ln else None

        def attn_apply(half, se_rows_ln2):
            cofs = half * 512
            TE.wait_ge(sSE.h, se_rows_ln2)
            TE.wait_ge(sDV.h, sDV.n)
            TE.matmul(PD[:, 0:512], ones_row[:, :], r_mean[0:1, cofs:cofs + 512],
                      start=True, stop=True)
            mm = TE.matmul(PD[:, 512:1024], ones_row[:, :], r_rstd[0:1, cofs:cofs + 512],
                           start=True, stop=True)
            pe = sPE.inc(mm)
            SEn.wait_ge(sPE.h, pe)
            SEn.wait_ge(sDV.h, sDV.n)
            SEn.copy(E3[:, :], PD[:, 0:512])
            rbtok = SEn.copy(E4[:, :], PD[:, 512:1024])
            SY.wait_ge(sDV.h, sDV.n)
            SY.wait_ge(sPO.h, sPO.n)
            toks = [STS[0].dma(SY, out=st_bufs[0][:, 0:512], in_=x1sp[0:128, cofs:cofs + 512]),
                    STS[1].dma(SY, out=st_bufs[1][:, 0:512], in_=x1sp[128:256, cofs:cofs + 512])]
            dv_tok = po_tok = None
            ptmp = E2 if half == 0 else S1
            for g in range(KD):
                if g == 0:
                    VE.wait_ge(sSE.h, rbtok)
                    PL.wait_ge(sSE.h, rbtok)
                    if half == 0:
                        OSL[0].wait(PL, OSL[0].s.n)
                        OSL[1].wait(PL, OSL[1].s.n)
                if g % 2 == 0:
                    STS[0].wait(VE, toks[g])
                    VE.tensor_tensor(E1[:, :], st_bufs[0][:, 0:512], E3[:, :], OP.subtract)
                    dv_tok = VE.tensor_tensor(
                        xn[:, g * 512:g * 512 + 512], E1[:, :], E4[:, :], OP.mult)
                else:
                    STS[1].wait(PL, toks[g])
                    PL.tensor_tensor(ptmp[:, 0:512], st_bufs[1][:, 0:512], E3[:, :], OP.subtract)
                    po_tok = PL.tensor_tensor(
                        xn[:, g * 512:g * 512 + 512], ptmp[:, 0:512], E4[:, :], OP.mult)
                if g + 2 < KD:
                    if g % 2 == 0:
                        SY.wait_ge(sDV.h, dv_tok)
                    else:
                        SY.wait_ge(sPO.h, po_tok)
                    toks.append(STS[g % 2].dma(
                        SY, out=st_bufs[g % 2][:, 0:512],
                        in_=x1sp[(g + 2) * 128:(g + 3) * 128, cofs:cofs + 512]))
            return (dv_tok, po_tok)

        def outproj_group(half, grp, y_token, first_wtk, W=128):
            cofs2 = half * 512 + grp * W
            wtk = first_wtk
            SY.wait_ge(sDV.h, sDV.n)
            xtk = [STS[0].dma(SY, out=st_bufs[0][:, 0:W], in_=x1sp[0:128, cofs2:cofs2 + W]),
                   STS[1].dma(SY, out=st_bufs[1][:, 0:W], in_=x1sp[128:256, cofs2:cofs2 + W])]
            op_mm, op_cp, op_evac, op_st = [], [], [], []
            for m2 in range(KD):
                pd_ = PB if (m2 % 2 == 0) else PC
                wosl[m2 % 2].wait(TE, wtk[m2])
                if m2 == 0:
                    TE.wait_ge(sPO.h, y_token)
                if m2 >= 2:
                    TE.wait_ge(sSE.h, op_cp[m2 - 2])
                mm = None
                for k in range(KD):
                    mm = TE.matmul(pd_[:, 0:W], wob[m2 % 2][:, k * 128:k * 128 + 128],
                                   y_bf[:, k * 512 + grp * W:k * 512 + grp * W + W],
                                   start=(k == 0), stop=(k == KD - 1))
                op_mm.append(sPE.inc(mm))
                if m2 + 2 < KD:
                    SEn.wait_ge(sPE.h, op_mm[m2])
                    wtk.append(wosl[m2 % 2].dma(SEn, out=wob[m2 % 2][:, :],
                                                in_=wo_b[m2 + 2, :, :]))
                ob = E2[:, (m2 % 2) * 256:(m2 % 2) * 256 + W]
                SEn.wait_ge(sPE.h, op_mm[m2])
                OSL[m2 % 2].wait(SEn, OSL[m2 % 2].s.n)
                op_cp.append(SEn.copy(ob, pd_[:, 0:W]))
                PL.wait_ge(sSE.h, op_cp[m2])
                STS[m2 % 2].wait(PL, xtk[m2])
                OSL[m2 % 2].wait(PL, OSL[m2 % 2].s.n)
                op_evac.append(PL.tensor_tensor(ob, ob, st_bufs[m2 % 2][:, 0:W], OP.add))
                if m2 + 2 < KD:
                    SY.wait_ge(sPO.h, op_evac[m2])
                    xtk.append(STS[m2 % 2].dma(
                        SY, out=st_bufs[m2 % 2][:, 0:W],
                        in_=x1sp[(m2 + 2) * 128:(m2 + 3) * 128, cofs2:cofs2 + W]))
                SEn.wait_ge(sPO.h, op_evac[m2])
                op_st.append(OSL[m2 % 2].dma(
                    SEn, out=x2sp[m2 * 128:(m2 + 1) * 128, cofs2:cofs2 + W], in_=ob))
            return op_evac[-1]

        def attn_half(half, se_rows_ln2, publish=False, carry=None):
            base_c = half * NCHH
            if half == 0:
                for m2 in range(KD):
                    CAT2.dma(PO, out=wo_b[m2, :, :], in_=wo_d[m2, :, :])
                dv_xnh = attn_apply(0, se_rows_ln2)
                kvab = [kvaA, kvaB]
                SY.wait_ge(CAT.s.h, CAT.s.n)
                SY.wait_ge(sPE.h, sPE.n)
                ktok = [KSL[0].dma(SY, out=kvab[0][:, :], in_=wkva_b[0, :, :])]
                qtok = [QSL.dma(SY, out=qbuf[:, :], in_=wq_b[0, :, :])]
                gtok = [GSL.dma(SY, out=gbuf[:, :], in_=wg_b[0, :, :])]
                prev_op = sPO.n
            else:
                dv_xnh = carry["dv_xnh"]
                kvab = [kvaA, kvaB]
                ktok = carry["ktok"]
                qtok = carry["qtok"]
                gtok = carry["gtok"]
                prev_op = carry["op_tail"]

            TE.wait_ge(CP.s.h, consts_p)
            SEn.wait_ge(CW.s.h, consts_w)
            ssq_mm, q2b_dv = [], []
            u_done, aclip_done, w1_done, w2_done = [], [], [], []
            free_k, free_a, free_g, free_q, free_v = [], [], [], [], []
            for h in range(H):
                kb = kvab[h % 2]
                KSL[h % 2].wait(TE, ktok[h])
                if h == 0:
                    TE.wait_ge(sDV.h, dv_xnh[0])
                    TE.wait_ge(sPO.h, dv_xnh[1])
                    TE.wait_ge(sPO.h, prev_op)
                else:
                    TE.wait_ge(sSE.h, free_k[h - 1])
                mm = None
                for k in range(KD):
                    mm = TE.matmul(PB[:, 0:512], kb[:, k * 128:k * 128 + 128],
                                   xn[:, k * 512:k * 512 + 512], start=(k == 0), stop=(k == KD - 1))
                kmm = sPE.inc(mm)
                if h > 0:
                    TE.wait_ge(sDV.h, free_v[h - 1])
                mm = None
                for k in range(KD):
                    mm = TE.matmul(PB[:, 512:1024], kb[:, 2048 + k * 128:2048 + k * 128 + 128],
                                   xn[:, k * 512:k * 512 + 512], start=(k == 0), stop=(k == KD - 1))
                vmm = sPE.inc(mm)
                if h > 0:
                    TE.wait_ge(sSE.h, free_a[h - 1])
                mm = None
                for k in range(KD):
                    mm = TE.matmul(PC[:, 0:512], kb[:, 4096 + k * 128:4096 + k * 128 + 128],
                                   xn[:, k * 512:k * 512 + 512], start=(k == 0), stop=(k == KD - 1))
                amm = sPE.inc(mm)
                if h + 1 < H:
                    SY.wait_ge(sPE.h, amm)
                    ktok.append(KSL[(h + 1) % 2].dma(SY, out=kvab[(h + 1) % 2][:, :],
                                                     in_=wkva_b[h + 1, :, :]))
                QSL.wait(TE, qtok[h])
                if h > 0:
                    TE.wait_ge(sDV.h, free_q[h - 1])
                mm = None
                for k in range(KD):
                    mm = TE.matmul(PA[:, 0:512], qbuf[:, k * 128:k * 128 + 128],
                                   xn[:, k * 512:k * 512 + 512], start=(k == 0), stop=(k == KD - 1))
                qmm = sPE.inc(mm)
                GSL.wait(TE, gtok[h])
                if h > 0:
                    TE.wait_ge(sSE.h, free_g[h - 1])
                mm = None
                for k in range(KD):
                    mm = TE.matmul(PA[:, 512:1024], gbuf[:, k * 128:k * 128 + 128],
                                   xn[:, k * 512:k * 512 + 512], start=(k == 0), stop=(k == KD - 1))
                gmm = sPE.inc(mm)
                if h > 0:
                    TE.wait_ge(sDV.h, q2b_dv[h - 1])
                    mm = TE.matmul(PD[0:16, 512:1024], ind16[:, (h - 1) * 16:h * 16], E5[:, :],
                                   start=(h - 1 == 0), stop=False)
                    ssq_mm.append(sPE.inc(mm))
                if h + 1 < H:
                    SY.wait_ge(sPE.h, qmm)
                    qtok.append(QSL.dma(SY, out=qbuf[:, :], in_=wq_b[h + 1, :, :]))
                    SY.wait_ge(sPE.h, gmm)
                    gtok.append(GSL.dma(SY, out=gbuf[:, :], in_=wg_b[h + 1, :, :]))
                SEn.wait_ge(sPE.h, kmm)
                if h == 0:
                    SEn.wait_ge(sDV.h, dv_xnh[0])
                else:
                    SEn.wait_ge(sDV.h, u_done[h - 1])
                tk = SEn.activation(E1[:, :], PB[:, 0:512], AF.Tanh)
                free_k.append(sSE.n)
                SEn.wait_ge(sPE.h, amm)
                if h == 0:
                    SEn.wait_ge(sPO.h, sPO.n)
                asig = SEn.activation(a_f32[:, h * 512:h * 512 + 512], PC[:, 0:512],
                                      AF.Sigmoid, bias=b2col[:, 0:1])
                free_a.append(sSE.n)
                SEn.wait_ge(sPE.h, gmm)
                if h == 0:
                    OSL[0].wait(SEn, OSL[0].s.n)
                    OSL[1].wait(SEn, OSL[1].s.n)
                    SEn.wait_ge(sPO.h, sPO.n)
                else:
                    SEn.wait_ge(sDV.h, w1_done[h - 1])
                gse = SEn.activation(E2[:, :], PA[:, 512:1024], AF.Sigmoid)
                if h > 0:
                    SEn.wait_ge(sDV.h, w2_done[h - 1])
                g2se = SEn.activation(E4[:, :], PA[:, 512:1024], AF.Sigmoid, scale=-1.0)
                free_g.append(sSE.n)
                VE.wait_ge(sPE.h, qmm)
                if h > 0:
                    VE.wait_ge(sDV.h, q2b_dv[h - 1])
                    VE.wait_ge(sPE.h, ssq_mm[h - 1])
                qcp = VE.tensor_copy(E3[:, :], PA[:, 0:512])
                free_q.append(sDV.n)
                VE.wait_ge(sSE.h, tk)
                VE.wait_ge(sPE.h, vmm)
                u_done.append(VE.tensor_tensor(
                    u_bf[:, h * 512:h * 512 + 512], E1[:, :], PB[:, 512:1024], OP.mult))
                VE.wait_ge(sSE.h, g2se)
                w2_done.append(VE.tensor_tensor(
                    w2_bf[:, h * 512:h * 512 + 512], E4[:, :], PB[:, 512:1024], OP.mult))
                free_v.append(sDV.n)
                VE.wait_ge(sSE.h, gse)
                w1_done.append(VE.tensor_tensor(
                    w1_bf[:, h * 512:h * 512 + 512], E2[:, :], E3[:, :], OP.mult))
                q2b_dv.append(VE.tensor_tensor(E5[:, :], E3[:, :], E3[:, :], OP.mult))
                VE.wait_ge(sSE.h, asig)
                aclip_done.append(VE.tensor_scalar(
                    a_f32[:, h * 512:h * 512 + 512], a_f32[:, h * 512:h * 512 + 512],
                    0.9995, 0.6, OP.min, OP.max))
            TE.wait_ge(sDV.h, q2b_dv[H - 1])
            mm = TE.matmul(PD[0:16, 512:1024], ind16[:, (H - 1) * 16:H * 16], E5[:, :],
                           start=False, stop=True)
            ssq_last = sPE.inc(mm)

            VE.wait_ge(sPE.h, ssq_last)
            VE.tensor_copy(RQ16[:, :], PD[0:16, 512:1024])
            VE.tensor_scalar(RQ16[:, :], RQ16[:, :], 1.0 / 128, RMSEPS, OP.mult, OP.add)
            SEn.wait_ge(sDV.h, sDV.n)
            SEn.activation(S3[0:16, 0:512], RQ16[:, :], AF.Ln)
            SEn.activation(RQ16[:, :], S3[0:16, 0:512], AF.Exp, scale=-0.5)
            VE.wait_ge(sSE.h, sSE.n)
            rqb_dv = VE.tensor_copy(RQ16b[:, :], RQ16[:, :])
            bcast_mm, w1rq = [], []
            for h in range(H):
                if h == 0:
                    TE.wait_ge(sDV.h, rqb_dv)
                if h >= 2:
                    TE.wait_ge(sDV.h, w1rq[h - 2])
                mm = TE.matmul(PA[:, (h % 2) * 512:(h % 2) * 512 + 512],
                               indr[:, h * 128:h * 128 + 128], RQ16b[:, :], start=True, stop=True)
                bcast_mm.append(sPE.inc(mm))
                VE.wait_ge(sPE.h, bcast_mm[h])
                w1rq.append(VE.tensor_tensor(
                    w1_bf[:, h * 512:h * 512 + 512], w1_bf[:, h * 512:h * 512 + 512],
                    PA[:, (h % 2) * 512:(h % 2) * 512 + 512], OP.mult))

            out = {}
            if half == 0:
                out["dv_xnh"] = attn_apply(1, se_rows_ln2)
                SY.wait_ge(sPE.h, sPE.n)
                out["ktok"] = [KSL[0].dma(SY, out=kvaA[:, :], in_=wkva_b[0, :, :])]
            S2r = S2[:, :].rearrange("p (h t) -> p h t", h=H)
            S3r = S3[:, :].rearrange("p (h t) -> p h t", h=H)
            S4r = S4[:, :].rearrange("p (h t) -> p h t", h=H)
            a3 = a_f32.rearrange("p (h t) -> p h t", h=H)
            u3 = u_bf.rearrange("p (h t) -> p h t", h=H)
            w13 = w1_bf.rearrange("p (h t) -> p h t", h=H)
            w23 = w2_bf.rearrange("p (h t) -> p h t", h=H)
            y3 = y_bf.rearrange("p (h t) -> p h t", h=H)
            R3 = R_bf.rearrange("p (h t) -> p h t", h=H)
            scol_edge = sDV.n
            last_comb = None
            op_tail = None
            pool_y = [0] * NCHH
            pool_pf = [0] * NCHH
            SEn.wait_ge(sPE.h, sPE.n)
            SEn.wait_ge(CAT2.s.h, CAT2.s.n)
            wtk = [wosl[0].dma(SEn, out=wob[0][:, :], in_=wo_b[0, :, :]),
                   wosl[1].dma(SEn, out=wob[1][:, :], in_=wo_b[1, :, :])]
            RV = nc.vector
            for cc in range(NCHH):
                c = base_c + cc
                Sp = [S1, S5][cc % 2]
                Spr = Sp[:, :].rearrange("p (h t) -> p h t", h=H)
                if cc == 0:
                    VE.wait_ge(sDV.h, w1rq[-1])
                    VE.wait_ge(OSL[0].s.h, OSL[0].s.n)
                    VE.wait_ge(OSL[1].s.h, OSL[1].s.n)
                    VE.wait_ge(sPO.h, sPO.n)
                if cc >= 2 and (base_c + cc - 2) < NCORR:
                    RV.wait_ge(sPO.h, pool_pf[cc - 2])
                for hh in range(H):
                    o = hh * 512 + cc * 64
                    ins = RV.tensor_tensor_scan(Sp[:, hh * 64:hh * 64 + 64],
                                                a_f32[:, o:o + 64], a_f32[:, o:o + 64],
                                                1.0, OP.mult, OP.bypass)
                    if hh == H - 1:
                        sDV.inc(ins)
                VE.tensor_scalar(S3[:, :], Sp[:, :], 1e6, 1.0, OP.mult, OP.min)
                t_tok = VE.tensor_tensor(S3r[:, :, :], u3[:, :, cc * 64:cc * 64 + 64],
                                         S3r[:, :, :], OP.mult)
                Std = [S4, S2][cc % 2]
                Stdr = [S4r, S2r][cc % 2]
                RV.wait_ge(sDV.h, t_tok)
                RV.wait_ge(sDV.h, scol_edge)
                if cc > 1:
                    RV.wait_ge(sPO.h, pool_y[cc - 2])
                for hh in range(H):
                    o = hh * 512 + cc * 64
                    ins = RV.tensor_tensor_scan(Std[:, hh * 64:hh * 64 + 64],
                                                a_f32[:, o:o + 64], S3[:, hh * 64:hh * 64 + 64],
                                                scol[:, hh:hh + 1], OP.mult, OP.add)
                    if hh == H - 1:
                        sDV.inc(ins)
                Acur = [AcA, AcB2][c % 2]
                Anxt = [AcA, AcB2][(c + 1) % 2]
                scol_edge = VE.tensor_copy(scol[:, :].unsqueeze(2), Stdr[:, :, 63:64])
                if cc >= 1 and (c - 1) < NCORR:
                    VE.wait_ge(sPO.h, pool_pf[cc - 1])
                VE.tensor_tensor(Anxt[:, :].unsqueeze(2), Acur[:, :].unsqueeze(2),
                                 Spr[:, :, 63:64], OP.mult)
                PL.wait_ge(sDV.h, scol_edge)
                if c < NCORR:
                    AcBc = Acur[:, :].unsqueeze(2).broadcast_to([128, H, 64])
                    a3c = a3[:, :, cc * 64:cc * 64 + 64]
                    PL.tensor_tensor(a3c, Spr[:, :, :], AcBc, OP.mult)
                    pool_pf[cc] = PL.tensor_tensor(R3[:, :, c * 64:c * 64 + 64],
                                                   w13[:, :, cc * 64:cc * 64 + 64], a3c, OP.mult)
                PL.tensor_tensor(y3[:, :, cc * 64:cc * 64 + 64],
                                 w13[:, :, cc * 64:cc * 64 + 64], Stdr[:, :, :], OP.mult)
                last_comb = pool_y[cc] = PL.tensor_tensor(
                    y3[:, :, cc * 64:cc * 64 + 64], y3[:, :, cc * 64:cc * 64 + 64],
                    w23[:, :, cc * 64:cc * 64 + 64], OP.add)
                if cc == NCHH - 1 and publish:
                    PO.wait_ge(sDV.h, scol_edge + 1)
                    ccA = cc_in.ap()[0:1, :].rearrange("r (d h) -> (r d) h", d=128)
                    ccB = cc_in.ap()[1:2, :].rearrange("r (d h) -> (r d) h", d=128)
                    CP.dma(PO, out=ccA, in_=AcA[:, :])
                    pubtok = CP.dma(PO, out=ccB, in_=scol[:, :])
                    PO.wait_ge(CP.s.h, pubtok)
                    PO.collective_compute(
                        "AllGather", OP.bypass, replica_groups=[list(range(NCORES))],
                        ins=[cc_in.ap().opt()], outs=[cc_out.ap().opt()],
                    ).then_inc(sCC.h)
                    sCC.n += 1
                if cc == 3 or cc == NCHH - 1:
                    op_tail = outproj_group(half, cc // 4, last_comb, wtk, W=256)
                    if cc < NCHH - 1:
                        SEn.wait_ge(sPE.h, sPE.n)
                        wtk = [wosl[0].dma(SEn, out=wob[0][:, :], in_=wo_b[0, :, :]),
                               wosl[1].dma(SEn, out=wob[1][:, :], in_=wo_b[1, :, :])]
            if half == 0:
                SY.wait_ge(sPE.h, sPE.n)
                out["qtok"] = [QSL.dma(SY, out=qbuf[:, :], in_=wq_b[0, :, :])]
                out["gtok"] = [GSL.dma(SY, out=gbuf[:, :], in_=wg_b[0, :, :])]
            out["op_tail"] = op_tail
            return out

        if STAGE >= 1:
            wtok1 = [WSL[0].dma(PO, out=WB[:, 0:4096], in_=wgu[0, :, :]),
                     WSL[1].dma(PO, out=WB[:, 4096:8192], in_=wgu[1, :, :])]
        pe_ln1 = ln_stats_full(x_in)
        se_rows = ln_rows(pe_ln1)
        dv_bc = ln_bcast(se_rows)
        dv_xn = ln_apply_full(x_in, dv_bc)
        if STAGE == 0:
            PO.wait_ge(sDV.h, dv_xn)
            t1 = None
            for g in range(KD):
                t1 = CP.dma(PO, out=out_d[g * 128:(g + 1) * 128, :],
                            in_=xn[:, g * 1024:(g + 1) * 1024])
            PO.wait_ge(CP.s.h, t1)
            SEn.wait_ge(CP.s.h, t1)
        if STAGE >= 1:
            ffn1_dst = out_d if STAGE == 1 else x1sp
            cast_jobs = []
            if STAGE >= 2:
                for h in range(H):
                    cast_jobs.append((wkva_b[h, :, 0:3072], wkva[h, :, 0:3072]))
                    cast_jobs.append((wkva_b[h, :, 3072:6144], wkva[h, :, 3072:6144]))
                for h in range(H):
                    cast_jobs.append((wq_b[h, :, :], wq_d[h, :, :]))
                    cast_jobs.append((wg_b[h, :, :], wg_d[h, :, :]))
            pe_ln2 = ffn_phase(x_in, ffn1_dst, dv_xn, fuse_ln=(STAGE >= 2), wtok=wtok1,
                               casts=cast_jobs)
        if STAGE >= 2:
            se_rows2 = ln_rows(pe_ln2)
            r0 = attn_half(0, se_rows2)
            attn_half(1, se_rows2, publish=True, carry=r0)
            PO.wait_ge(sCC.h, sCC.n)
            gv = cc_out.ap().rearrange("r (d h) -> r d h", d=128).transpose([1, 0, 2])
            gtok = CP.dma(PO, out=gab[:, :].rearrange("p (r h) -> p r h", r=2 * NCORES), in_=gv)
            CP.wait(VE, gtok)
            for j in range(7):
                VE.tensor_scalar(T1[:, :], gab[:, 2 * j * 16:(2 * j + 1) * 16],
                                 mskT[:, j:j + 1], 0.0, OP.mult, OP.add)
                VE.tensor_scalar(T1[:, :], T1[:, :], mskT[:, 8 + j:9 + j], 0.0, OP.add, OP.add)
                VE.tensor_tensor(sin_t[:, :], sin_t[:, :], T1[:, :], OP.mult)
                VE.tensor_scalar(T2[:, :], gab[:, (2 * j + 1) * 16:(2 * j + 2) * 16],
                                 mskT[:, j:j + 1], 0.0, OP.mult, OP.add)
                VE.tensor_tensor(sin_t[:, :], sin_t[:, :], T2[:, :], OP.add)
            VE.wait_ge(sPO.h, sPO.n)
            sinB = sin_t[:, :].unsqueeze(2).broadcast_to([128, H, NCORR * 64])
            R3f = R_bf.rearrange("p (h t) -> p h t", h=H)
            c3 = corr_bf.rearrange("p (h t) -> p h t", h=H)
            corr_dv = (VE.tensor_tensor(c3[:, :, :], R3f[:, :, :], sinB, OP.mult))
            SEn.wait_ge(sPE.h, sPE.n)
            wtk2 = [wosl[0].dma(SEn, out=wob[0][:, :], in_=wo_b[0, :, :]),
                    wosl[1].dma(SEn, out=wob[1][:, :], in_=wo_b[1, :, :])]
            SY.wait_ge(sDV.h, sDV.n)
            xtk2 = [STS[0].dma(SY, out=st_bufs[0][:, 0:256], in_=x2sp[0:128, 0:256]),
                    STS[1].dma(SY, out=st_bufs[1][:, 0:256], in_=x2sp[128:256, 0:256])]
            op2_mm, op2_evac, op2_st = [], [], []
            for m2 in range(KD):
                pd_ = PB if (m2 % 2 == 0) else PC
                wosl[m2 % 2].wait(TE, wtk2[m2])
                if m2 == 0:
                    TE.wait_ge(sDV.h, corr_dv)
                if m2 >= 2:
                    TE.wait_ge(sDV.h, op2_evac[m2 - 2])
                mm = None
                for k in range(KD):
                    mm = TE.matmul(pd_[:, 0:256], wob[m2 % 2][:, k * 128:k * 128 + 128],
                                   corr_bf[:, k * 256:k * 256 + 256], start=(k == 0), stop=(k == KD - 1))
                op2_mm.append(sPE.inc(mm))
                if m2 + 2 < KD:
                    SEn.wait_ge(sPE.h, op2_mm[m2])
                    wtk2.append(wosl[m2 % 2].dma(SEn, out=wob[m2 % 2][:, :], in_=wo_b[m2 + 2, :, :]))
                ob = [S4, S5][m2 % 2]
                VE.wait_ge(sPE.h, op2_mm[m2])
                STS[m2 % 2].wait(VE, xtk2[m2])
                OSL[m2 % 2].wait(VE, OSL[m2 % 2].s.n)
                op2_evac.append((VE.tensor_tensor(
                    ob[:, 0:256], pd_[:, 0:256], st_bufs[m2 % 2][:, 0:256], OP.add)))
                if m2 + 2 < KD:
                    SY.wait_ge(sDV.h, op2_evac[m2])
                    xtk2.append(STS[m2 % 2].dma(
                        SY, out=st_bufs[m2 % 2][:, 0:256],
                        in_=x2sp[(m2 + 2) * 128:(m2 + 3) * 128, 0:256]))
                SEn.wait_ge(sDV.h, op2_evac[m2])
                op2_st.append(OSL[m2 % 2].dma(
                    SEn, out=x2sp[m2 * 128:(m2 + 1) * 128, 0:256], in_=ob[:, 0:256]))
        if STAGE == 2:
            SY.wait_ge(OSL[0].s.h, OSL[0].s.n)
            SY.wait_ge(OSL[1].s.h, OSL[1].s.n)
            SY.wait_ge(sDV.h, sDV.n)
            btoks = []
            for g in range(KD):
                if g >= 2:
                    OSL[g % 2].wait(SY, btoks[g - 2][1])
                lt = STS[g % 2].dma(SY, out=st_bufs[g % 2][:, :],
                                    in_=x2sp[g * 128:(g + 1) * 128, :])
                STS[g % 2].wait(SEn, lt)
                stt = OSL[g % 2].dma(SEn, out=out_d[g * 128:(g + 1) * 128, :],
                                     in_=st_bufs[g % 2][:, :])
                btoks.append((lt, stt))
        if STAGE >= 3:
            SY.wait_ge(OSL[0].s.h, OSL[0].s.n)
            SY.wait_ge(OSL[1].s.h, OSL[1].s.n)
            PO.wait_ge(sPE.h, sPE.n)
            wtok2 = [WSL[0].dma(PO, out=WB[:, 0:4096], in_=wgu[0, :, :]),
                     WSL[1].dma(PO, out=WB[:, 4096:8192], in_=wgu[1, :, :])]
            pe_ln3 = ln_stats_full(x2sp)
            se_rows3 = ln_rows(pe_ln3)
            dv_bc3 = ln_bcast(se_rows3)
            dv_xn3 = ln_apply_full(x2sp, dv_bc3)
            ffn_phase(x2sp, out_d, dv_xn3, fuse_ln=False, wtok=wtok2)
        SEn.wait_ge(OSL[0].s.h, OSL[0].s.n)
        SEn.wait_ge(OSL[1].s.h, OSL[1].s.n)
    es.close()
    return nc


def _tiles(w):
    O, I = w.shape
    nm, nk = O // 128, I // 128
    t = w.reshape(nm, 128, nk, 128).transpose(0, 3, 2, 1)
    return np.ascontiguousarray(t).reshape(nm, 128, I)


def kernel(L0_attn_x, L0_attn_w_in, L0_attn_w_out, L0_ffn_gate_w, L0_ffn_up_w,
           L0_ffn_down_w, L0_ffn_gate_s, L0_ffn_up_s, L0_ffn_down_s):
    x = np.ascontiguousarray(np.asarray(L0_attn_x, dtype=np.float32))
    w_in = np.asarray(L0_attn_w_in, dtype=np.float32)
    w_out = np.asarray(L0_attn_w_out, dtype=np.float32)
    gate_w = np.asarray(L0_ffn_gate_w, dtype=np.float32)
    up_w = np.asarray(L0_ffn_up_w, dtype=np.float32)
    down_w = np.asarray(L0_ffn_down_w, dtype=np.float32)
    gate_s = np.asarray(L0_ffn_gate_s, dtype=np.float32)
    up_s = np.asarray(L0_ffn_up_s, dtype=np.float32)
    down_s = np.asarray(L0_ffn_down_s, dtype=np.float32)

    tg = _tiles(gate_w)
    tu = _tiles(up_w)
    wgu_h = np.ascontiguousarray(np.concatenate([tg, tu], axis=2))
    wd_h = _tiles(down_w)
    wit = _tiles(w_in)
    wkva_h = np.ascontiguousarray(np.concatenate([wit[16:32], wit[32:48], wit[48:64]], axis=2))
    wq_h = np.ascontiguousarray(wit[0:16])
    wg_h = np.ascontiguousarray(wit[64:80])
    wo_h = _tiles(w_out)
    gs2 = np.ascontiguousarray(gate_s.reshape(KI, 128).T)
    us2 = np.ascontiguousarray(up_s.reshape(KI, 128).T)
    ds2 = np.ascontiguousarray(down_s.reshape(KD, 128).T)
    ind16_h = np.zeros((128, 16 * H), dtype=np.float32)
    for h in range(H):
        ind16_h[:, h * 16 + h] = 1.0
    indr_h = np.zeros((16, 128 * H), dtype=np.float32)
    for h in range(H):
        indr_h[h, h * 128:(h + 1) * 128] = 1.0

    nc = build_program()
    in_maps = []
    for c in range(NCORES):
        b, q = c // 4, c % 4
        xs = np.ascontiguousarray(x[b, q * TPC:(q + 1) * TPC, :].T)
        msk = np.zeros((128, 16), dtype=np.float32)
        for j in range(8):
            beta = 1.0 if (j // 4 == b and j < c) else 0.0
            msk[:, j] = beta
            msk[:, 8 + j] = 1.0 - beta
        in_maps.append({
            "x": xs, "wgu": wgu_h, "wd": wd_h, "wkva": wkva_h, "wq": wq_h, "wgp": wg_h,
            "wo": wo_h, "gs": gs2, "us": us2, "ds": ds2, "msk": msk,
            "ind16": ind16_h, "indr": indr_h,
        })
    res = run_bass_kernel_spmd(nc, in_maps, core_ids=list(range(NCORES)))
    out = np.empty((B, S, D), dtype=np.float32)
    for c in range(NCORES):
        b, q = c // 4, c % 4
        out[b, q * TPC:(q + 1) * TPC, :] = res.results[c]["out"].T
    return out
